# version 23
# baseline (speedup 1.0000x reference)
import os

import numpy as np

import concourse.bass as bass
import concourse.mybir as mybir
from concourse.bass_utils import run_bass_kernel_spmd

F32 = mybir.dt.float32
BF16 = mybir.dt.bfloat16
AF = mybir.ActivationFunctionType
OP = mybir.AluOpType
AX = mybir.AxisListType

DBG = bool(int(os.environ.get("MK_DEBUG", "0")))

D = 2048
S = 2048
NT = 16
NTH = 8
TW = 1024
OC = 16
DC = 16
EN = 8
PI = float(np.pi)
LOG1E4 = float(np.log(10000.0))
BIG = 1.0e30

ENGS = ("sync", "tensor", "vector", "scalar", "gpsimd")
ESEM = {"sync": "dio", "tensor": "pe", "vector": "dve", "scalar": "act",
        "gpsimd": "dw"}
DMASEMS = ("dio", "dw")


class Ev:
    __slots__ = ("sem", "val")

    def __init__(self, sem, val):
        self.sem = sem
        self.val = val


class Prog:

    def __init__(self):
        self.q = {e: [] for e in ENGS}
        self.counts = {"ph": 0}
        self.phase_n = 0
        self.bufw = {}
        self.bufr = {}

    def op(self, eng, fn, reads=(), writes=(), sem=None, inck=1):
        sem = sem or ESEM[eng]
        is_dma = sem in DMASEMS
        waits = []
        for b in reads:
            w = self.bufw.get(b)
            if w is not None:
                waits.append(w)
        for b in writes:
            w = self.bufw.get(b)
            if w is not None:
                waits.append(w)
            waits.extend(self.bufr.get(b, ()))
        pruned = {}
        for w in waits:
            wdma = w.sem in DMASEMS
            if w.sem == sem == "pe":
                continue
            if pruned.get(w.sem, -1) < w.val:
                pruned[w.sem] = w.val
        self.counts[sem] = self.counts.get(sem, 0) + inck
        ev = Ev(sem, self.counts[sem])
        self.q[eng].append(("op", fn, tuple(Ev(s, v) for s, v in pruned.items()),
                            sem, inck))
        for b in reads:
            self.bufr.setdefault(b, []).append(ev)
        for b in writes:
            self.bufw[b] = ev
            self.bufr[b] = []
        return ev

    def barrier(self):
        self.phase_n += 1
        self.counts["ph"] += len(ENGS)
        tgt = self.phase_n * len(ENGS)
        for e in ENGS:
            self.q[e].append(("bar", tgt))
        ev = Ev("ph", tgt)
        for b in list(self.bufw):
            self.bufw[b] = ev
        self.bufr = {}

    def burst_map(self):
        owner = {"dio": "sync", "dw": "gpsimd"}
        self.bm = {s: {} for s in owner}
        for s, eng in owner.items():
            run = []
            cum = 0
            for item in self.q[eng]:
                is_burst = (item[0] == "op" and item[4] == 16
                            and item[3] == s and not item[2])
                if item[0] == "op" and item[3] == s:
                    cum += item[4]
                if is_burst:
                    run.append(cum)
                else:
                    for v in run:
                        self.bm[s][v] = run[-1]
                    run = []
                    if item[0] == "op" and item[3] == s:
                        run = [cum]
            for v in run:
                self.bm[s][v] = run[-1]

    def emit(self, eng_name, engobj, sems):
        waited = {}
        for item in self.q[eng_name]:
            if item[0] == "bar":
                engobj.sem_inc(sems["ph"], 1)
                engobj.wait_ge(sems["ph"], item[1])
                waited = {"ph": item[1]}
                continue
            _, fn, waits, sem, inck = item
            for w in waits:
                val = self.bm.get(w.sem, {}).get(w.val, w.val)
                if waited.get(w.sem, -1) >= val:
                    continue
                engobj.wait_ge(sems[w.sem], val)
                waited[w.sem] = val
            res = fn(engobj)
            last = res[-1] if isinstance(res, (list, tuple)) else res
            last.then_inc(sems[sem], inck)


def build_program():
    nc = bass.Bass("TRN2", target_bir_lowering=False)

    def din(name, shape):
        return nc.dram_tensor(name, shape, F32, kind="ExternalInput").ap()

    x_attn = din("x_attn", [S, D])
    x_ffn = din("x_ffn", [TW, D])
    wq_b = din("wq_b", [OC, 128, DC, 128])
    wk_b = din("wk_b", [OC, 128, DC, 128])
    wv_b = din("wv_b", [OC, 128, DC, 128])
    wo_b = din("wo_b", [DC, 128, OC, 128])
    w3t_d = din("w3t", [DC, 128, DC, 128])
    w1t_d = din("w1t", [128, EN, 128])
    w2t_d = din("w2t", [128, EN, 128])
    sw_d = din("sw_b", [EN, NT, 128, D])
    bq_d = din("bq_c", [128, OC])
    bk_d = din("bk_c", [128, OC])
    bv_d = din("bv_c", [128, OC])
    bo_d = din("bo_c", [128, DC])
    b3_d = din("b3_c", [128, DC])
    b1_d = din("b1_c", [128, EN])
    b2t_d = din("b2t", [EN, 128])
    swb_d = din("swb", [1, EN])
    pos_d = din("pos", [1, S])
    posq_d = din("posq", [1, TW])
    parc_d = din("parc", [128, 1])
    out_d = nc.dram_tensor("out", [NTH, 128, D], F32, kind="ExternalOutput").ap()
    if DBG:
        dbg_qp = nc.dram_tensor("dbg_qp", [OC, 128, TW], F32,
                                kind="ExternalOutput").ap()
        dbg_kp = nc.dram_tensor("dbg_kp", [OC, NT, 128, 128], F32,
                                kind="ExternalOutput").ap()
        dbg_sv = nc.dram_tensor("dbg_sv", [OC, NT, 128, 128], F32,
                                kind="ExternalOutput").ap()
        dbg_xn = nc.dram_tensor("dbg_xn", [NT, 128, D], F32,
                                kind="ExternalOutput").ap()
        dbg_yb = nc.dram_tensor("dbg_yb", [OC, 128, TW], F32,
                                kind="ExternalOutput").ap()
        dbg_s = nc.dram_tensor("dbg_s", [DC, 128, TW], F32,
                               kind="ExternalOutput").ap()
        dbg_g = nc.dram_tensor("dbg_g", [DC, 128, TW], F32,
                               kind="ExternalOutput").ap()
        dbg_u = nc.dram_tensor("dbg_u", [DC, 128, TW], F32,
                               kind="ExternalOutput").ap()
        dbg_sc = nc.dram_tensor("dbg_sc", [128, 256], F32,
                                kind="ExternalOutput").ap()
        dbg_tab = nc.dram_tensor("dbg_tab", [4, 128, S], F32,
                                 kind="ExternalOutput").ap()
        dbg_pm = nc.dram_tensor("dbg_pm", [2, 128, 128], F32,
                                kind="ExternalOutput").ap()

    kp_dram = nc.dram_tensor("kp_dram", [OC, NT, 128, 128], BF16).ap()
    svt_dram = nc.dram_tensor("svt_dram", [OC, NT, 128, 130], BF16).ap()
    xt_dram = nc.dram_tensor("xt_dram", [NT, 128, D], BF16).ap()
    i32_dram = nc.dram_tensor("i32_dram", [32, 32], BF16).ap()

    p = Prog()
    live = {}

    def salloc(name, shape, dtype, side="left"):
        cm = nc.sbuf_tensor(name, shape, dtype, side=side)
        h = cm.__enter__()
        live[name] = cm
        return h

    def sfree(*names):
        for name in names:
            live.pop(name).__exit__(None, None, None)

    cos_t = salloc("cos_t", [128, S], BF16)
    sin_t = salloc("sin_t", [128, S], BF16)
    cos_q = salloc("cos_q", [128, TW], BF16)
    sin_q = salloc("sin_q", [128, TW], BF16)
    i128 = salloc("i128", [128, 128], BF16)
    pmat = salloc("pmat", [128, 128], BF16)
    tril = salloc("tril", [128, 128], BF16)
    am = salloc("am", [128, 2, 128], BF16)
    ones_col = salloc("ones_col", [128, 1], BF16)
    ones_row = salloc("ones_row", [1, 128], BF16)
    ones_row_f = salloc("ones_row_f", [1, 128], F32)
    negpi = salloc("negpi", [128, 1], F32)
    epsc = salloc("epsc", [128, 1], F32)
    parc = salloc("parcs", [128, 1], F32)
    bqs = salloc("bqs", [128, OC], F32)
    bks = salloc("bks", [128, OC], F32)
    bvh = salloc("bvh", [128, OC], F32)
    bos = salloc("bos", [128, DC], F32)
    b3s = salloc("b3s", [128, DC], F32)
    b1s = salloc("b1s", [128, EN], F32)
    b2ts = salloc("b2ts", [EN, 128], F32)
    swbs = salloc("swbs", [1, EN], F32)

    mcols = salloc("mcols", [128, OC], F32)
    lnsc = salloc("lnsc", [128, 12], F32)
    msc = salloc("msc", [128, 32], F32)
    mscb = salloc("mscb", [128, 32], BF16)
    mrow = salloc("mrow", [1, 64], F32)
    lgrow = salloc("lgrow", [1, EN], F32)
    probs = salloc("probs", [1, EN], F32)
    pscr = salloc("pscr", [1, EN], F32)
    psc2 = salloc("psc2", [1, 4], F32)
    gates = salloc("gates", [1, EN], F32)
    gbc = salloc("gbc", [128, EN], F32)
    gcol = salloc("gcol", [EN, 1], F32)
    b2g = salloc("b2g", [128, 1], F32)
    rr = salloc("rr", [1, 512], BF16)

    xn_ffn = salloc("xn_ffn", [128, DC, TW], BF16)
    qp = salloc("qp", [128, OC, TW], BF16)

    def palloc(name, shape, dtype):
        cm = nc.psum_tensor(name, shape, dtype)
        h = cm.__enter__()
        live[name] = cm
        return h

    pb = [palloc(f"pb{i}", [128, 512], F32) for i in range(6)]
    pT = palloc("pT", [128, 1024], BF16)
    pS = palloc("pS", [128, 512], F32)

    p.op("gpsimd", lambda g: g.iota(i128[:, :], [[1, 128]], channel_multiplier=-1,
                                    base=0, allow_small_or_imprecise_dtypes=True),
         writes=["i128"], sem="pool")
    p.op("vector", lambda v: v.tensor_single_scalar(i128[:, :], i128[:, :], 0.0,
                                                    OP.is_equal),
         reads=["i128"], writes=["i128"])
    p.op("gpsimd", lambda g: g.iota(tril[:, :], [[1, 128]], channel_multiplier=-1,
                                    base=0, allow_small_or_imprecise_dtypes=True),
         writes=["tril"], sem="pool")
    p.op("vector", lambda v: v.tensor_single_scalar(tril[:, :], tril[:, :], 0.0,
                                                    OP.is_ge),
         reads=["tril"], writes=["tril"])
    p.op("vector", lambda v: v.memset(ones_col[:, :], 1.0), writes=["ones_col"])
    p.op("vector", lambda v: v.memset(ones_row[:, :], 1.0), writes=["ones_row"])
    p.op("vector", lambda v: v.memset(ones_row_f[:, :], 1.0),
         writes=["ones_row_f"])
    p.op("vector", lambda v: v.memset(negpi[:, :], -PI), writes=["negpi"])
    p.op("vector", lambda v: v.memset(epsc[:, :], 1e-5), writes=["epsc"])

    def ld(dstname, dst, src):
        return p.op("sync", lambda s: s.dma_start(out=dst, in_=src),
                    writes=[dstname], sem="dio", inck=16)

    ld("bqs", bqs[:, :], bq_d)
    ld("bks", bks[:, :], bk_d)
    ld("bvh", bvh[:, :], bv_d)
    ld("bos", bos[:, :], bo_d)
    ld("b3s", b3s[:, :], b3_d)
    ld("b1s", b1s[:, :], b1_d)
    ld("b2ts", b2ts[:, :], b2t_d)
    ld("swbs", swbs[:, :], swb_d)
    ld("parcs", parc[:, :], parc_d)
    p.op("vector", lambda v: v.tensor_single_scalar(bvh[:, :], bvh[:, :], 0.5,
                                                    OP.mult),
         reads=["bvh"], writes=["bvh"])
    p.op("vector", lambda v: v.tensor_scalar(am[:, 0, :], tril[:, :],
                                             parc[:, 0:1], None, OP.max),
         reads=["tril", "parcs"], writes=["am"])
    p.op("vector", lambda v: v.tensor_scalar(am[:, 1, :], tril[:, :],
                                             parc[:, 0:1], None, OP.min),
         reads=["tril", "parcs"], writes=["am"])

    p.op("vector", lambda v: v.memset(pmat[:, :], 0.0), writes=["pmat"])
    p.op("sync", lambda s: s.dma_start(out=i32_dram, in_=i128[0:32, 0:32]),
         reads=["i128"], writes=["i32d"], sem="dio", inck=16)
    for (pr, pc) in ((0, 32), (32, 0), (64, 96), (96, 64)):
        p.op("sync",
             lambda s, pr=pr, pc=pc: s.dma_start(
                 out=pmat[pr:pr + 32, pc:pc + 32], in_=i32_dram),
             reads=["i32d"], writes=["pmat"], sem="dio", inck=16)

    poss = salloc("poss", [1, S], F32, side="right")
    possq = salloc("possq", [1, TW], F32, side="right")
    ld("poss", poss[:, :], pos_d)
    ld("possq", possq[:, :], posq_d)
    invf = salloc("invf", [1, 128], F32, side="right")
    sgn = salloc("sgn", [128, 1], F32, side="right")
    sgr = salloc("sgr", [1, 128], F32, side="right")
    ang = salloc("ang", [128, S], F32, side="right")
    aki = salloc("aki", [128, 512], mybir.dt.int32, side="right")
    akf = salloc("akf", [128, 512], F32, side="right")
    akm = salloc("akm", [128, 512], F32, side="right")
    p.op("gpsimd", lambda g: g.iota(invf[:, :], [[0, 4], [1, 32]],
                                    channel_multiplier=0, base=0,
                                    allow_small_or_imprecise_dtypes=True),
         writes=["invf"], sem="pool")
    p.op("scalar", lambda sc: sc.activation(invf[:, :], invf[:, :], AF.Exp,
                                            bias=0.0, scale=-LOG1E4 / 32.0),
         reads=["invf"], writes=["invf"])
    p.op("gpsimd", lambda g: g.iota(sgr[:, :], [[0, 2], [1, 64]],
                                    channel_multiplier=0, base=0,
                                    allow_small_or_imprecise_dtypes=True),
         writes=["sgr"], sem="pool")
    p.op("vector", lambda v: v.tensor_single_scalar(sgr[:, :], sgr[:, :], 32.0,
                                                    OP.is_lt),
         reads=["sgr"], writes=["sgr"])
    p.op("vector", lambda v: v.tensor_scalar(sgr[:, :], sgr[:, :], -2.0, 1.0,
                                             OP.mult, OP.add),
         reads=["sgr"], writes=["sgr"])
    p.op("tensor", lambda t: t.matmul(pS[:, 0:1], sgr[0:1, :],
                                      ones_row_f[0:1, 0:1], start=True,
                                      stop=True),
         reads=["sgr", "ones_row_f"], writes=["pS"])
    p.op("vector", lambda v: v.tensor_copy(sgn[:, :], pS[:, 0:1]),
         reads=["pS"], writes=["sgn"])

    def angles(shift, posrow, posname, nw, dstname, dst):
        for ch in range(nw):
            sl = slice(ch * 512, ch * 512 + 512)
            p.op("tensor",
                 lambda t, sl=sl, ch=ch: t.matmul(pb[ch % 4][:, :], invf[0:1, :],
                                                  posrow[0:1, sl], start=True,
                                                  stop=True),
                 reads=["invf", posname], writes=[f"pb{ch % 4}"])
            p.op("vector", lambda v, ch=ch: v.tensor_scalar(
                akf[:, :], pb[ch % 4][:, :], shift, 1.0 / (2 * PI), OP.add,
                OP.mult),
                 reads=[f"pb{ch % 4}"], writes=["akf"])
            p.op("vector", lambda v: v.tensor_copy(aki[:, :], akf[:, :]),
                 reads=["akf"], writes=["aki"])
            p.op("vector", lambda v: v.tensor_copy(akf[:, :], aki[:, :]),
                 reads=["aki"], writes=["akf"])
            p.op("vector", lambda v, ch=ch: v.scalar_tensor_tensor(
                akf[:, :], akf[:, :], -2 * PI, pb[ch % 4][:, :], OP.mult,
                OP.add),
                 reads=["akf", f"pb{ch % 4}"], writes=["akf"])
            p.op("vector", lambda v: v.tensor_single_scalar(
                akf[:, :], akf[:, :], shift, OP.add),
                 reads=["akf"], writes=["akf"])
            p.op("vector", lambda v: v.tensor_single_scalar(
                akm[:, :], akf[:, :], PI, OP.is_gt),
                 reads=["akf"], writes=["akm"])
            p.op("vector", lambda v: v.scalar_tensor_tensor(
                akf[:, :], akm[:, :], -2 * PI, akf[:, :], OP.mult, OP.add),
                 reads=["akm", "akf"], writes=["akf"])
            p.op("vector", lambda v: v.tensor_single_scalar(
                akm[:, :], akf[:, :], -PI, OP.is_lt),
                 reads=["akf"], writes=["akm"])
            p.op("vector", lambda v, sl=sl: v.scalar_tensor_tensor(
                ang[:, sl], akm[:, :], 2 * PI, akf[:, :], OP.mult, OP.add),
                 reads=["akm", "akf"], writes=["ang"])
        p.op("scalar", lambda sc, nw=nw: sc.activation(
            dst[:, 0:nw * 512], ang[:, 0:nw * 512], AF.Sin, bias=0.0,
            scale=1.0),
             reads=["ang"], writes=[dstname])

    angles(0.0, poss, "poss", 4, "sin_t", sin_t)
    angles(PI / 2, poss, "poss", 4, "cos_t", cos_t)
    angles(0.0, possq, "possq", 2, "sin_q", sin_q)
    angles(PI / 2, possq, "possq", 2, "cos_q", cos_q)
    p.op("vector", lambda v: v.tensor_scalar(sin_t[:, :], sin_t[:, :],
                                             sgn[:, 0:1], None, OP.mult),
         reads=["sin_t", "sgn"], writes=["sin_t"])
    p.op("vector", lambda v: v.tensor_scalar(sin_q[:, :], sin_q[:, :],
                                             sgn[:, 0:1], None, OP.mult),
         reads=["sin_q", "sgn"], writes=["sin_q"])
    sfree("akm", "akf", "aki", "ang", "sgr", "sgn", "invf",
          "possq", "poss")
    p.barrier()

    xn_fm = salloc("xn_fm", [128, DC, S], BF16, side="right")
    xt = [salloc(f"xt{i}", [128, D], F32, side="right") for i in range(2)]
    xnt = salloc("xnt", [128, D], BF16, side="right")
    xsq = salloc("xsq", [128, D], BF16, side="right")

    def emit_ln(dstname, dst_fm, src_tiled, ntiles, to_dram):
        for i in range(ntiles):
            bi = i % 2
            bn = f"xt{bi}"
            c0 = 6 * bi
            p.op("sync", lambda s, i=i, bi=bi: s.dma_start(out=xt[bi][:, :],
                                                           in_=src_tiled[i]),
                 writes=[bn], sem="dio", inck=16)
            p.op("vector", lambda v, bi=bi, c0=c0: v.reduce_sum(
                lnsc[:, c0:c0 + 1], xt[bi][:, :], AX.X),
                 reads=[bn], writes=[f"ls{bi}"])
            p.op("scalar", lambda sc, bi=bi, c0=c0: sc.activation(
                xsq[:, :], xt[bi][:, :], AF.Square,
                accum_out=lnsc[:, c0 + 1:c0 + 2]),
                 reads=[bn], writes=["xsq", f"lq{bi}"])
            p.op("vector", lambda v, c0=c0: v.tensor_single_scalar(
                lnsc[:, c0:c0 + 1], lnsc[:, c0:c0 + 1], 1.0 / D, OP.mult),
                 reads=[f"ls{bi}"], writes=[f"ls{bi}"])
            p.op("vector", lambda v, c0=c0: v.tensor_mul(
                lnsc[:, c0 + 2:c0 + 3], lnsc[:, c0:c0 + 1],
                lnsc[:, c0:c0 + 1]),
                 reads=[f"ls{bi}"], writes=[f"lm{bi}"])
            p.op("vector", lambda v, c0=c0: v.scalar_tensor_tensor(
                lnsc[:, c0 + 3:c0 + 4], lnsc[:, c0 + 1:c0 + 2], 1.0 / D,
                lnsc[:, c0 + 2:c0 + 3], OP.mult, OP.subtract),
                 reads=[f"lq{bi}", f"lm{bi}"], writes=[f"lv{bi}"])
            p.op("scalar", lambda sc, c0=c0: sc.activation(
                lnsc[:, c0 + 4:c0 + 5], lnsc[:, c0 + 3:c0 + 4], AF.Sqrt,
                bias=epsc[:, 0:1], scale=1.0),
                 reads=[f"lv{bi}", "epsc"], writes=[f"lsd{bi}"])
            p.op("vector", lambda v, c0=c0: v.reciprocal(
                lnsc[:, c0 + 5:c0 + 6], lnsc[:, c0 + 4:c0 + 5]),
                 reads=[f"lsd{bi}"], writes=[f"lr{bi}"])
            p.op("vector", lambda v, bi=bi, c0=c0: v.tensor_scalar(
                xnt[:, :], xt[bi][:, :], lnsc[:, c0:c0 + 1],
                lnsc[:, c0 + 5:c0 + 6], OP.subtract, OP.mult),
                 reads=[bn, f"ls{bi}", f"lr{bi}"], writes=["xnt"])
            if to_dram:
                p.op("sync", lambda s, i=i: s.dma_start(out=xt_dram[i],
                                                        in_=xnt[:, :]),
                     reads=["xnt"], writes=["xt_dram"], sem="dio", inck=16)
            for half in range(2):
                for j in range(8):
                    dc = half * 8 + j
                    p.op("tensor", lambda t, dc=dc, j=j: t.transpose(
                        pT[:, j * 128:(j + 1) * 128],
                        xnt[:, dc * 128:(dc + 1) * 128], i128[:, :]),
                         reads=["xnt", "i128"], writes=["pT"])
                p.op("vector", lambda v, i=i, half=half: v.tensor_copy(
                    dst_fm[:, half * 8:half * 8 + 8, i * 128:(i + 1) * 128],
                    pT[:, :].rearrange("p (dc t) -> p dc t", t=128)),
                     reads=["pT"], writes=[dstname])

    emit_ln("xn_fm", xn_fm, x_attn.rearrange("(n p) d -> n p d", p=128), NT,
            True)
    emit_ln("xn_ffn", xn_ffn, x_ffn.rearrange("(n p) d -> n p d", p=128), NTH,
            False)
    sfree("xsq", "xnt", "xt1", "xt0")
    p.barrier()

    wqs1 = salloc("wqs0", [128, DC, 128], BF16, side="right")
    wks1 = salloc("wks0", [128, DC, 128], BF16, side="right")
    wvs = salloc("wvs", [128, DC, 128], BF16, side="right")
    wqs = [wqs1, wqs1]
    wks = [wks1, wks1]
    ra = salloc("ra", [128, S], BF16, side="right")
    rb = salloc("rb", [128, S], BF16, side="right")
    kpb = salloc("kpb", [128, S], BF16, side="right")
    svq = salloc("svq", [128, 8, 130], BF16, side="right")

    def wload(dst, dstname, src):
        p.op("gpsimd", lambda g: g.dma_start(out=dst[:, :, :], in_=src),
             writes=[dstname], sem="dw", inck=16)

    p.op("vector", lambda v: v.memset(svq[:, :, 64:65], 1.0), writes=["svqo"])
    p.op("vector", lambda v: v.memset(svq[:, :, 129:130], 1.0), writes=["svqo"])

    def rope(src, srcname, cosname, cost, sinname, sint, dst_fn, dstname, nw):
        for tch in range(nw):
            sl = slice(tch * 512, tch * 512 + 512)
            rbk = 4 + tch % 2
            p.op("tensor", lambda t, sl=sl, rbk=rbk: t.matmul(
                pb[rbk][:, :], pmat[:, :], src[:, sl], start=True, stop=True),
                 reads=[srcname, "pmat"], writes=[f"pb{rbk}"])
            p.op("vector", lambda v, sl=sl, rbk=rbk: v.tensor_mul(
                rb[:, sl], pb[rbk][:, :], sint[:, sl]),
                 reads=[f"pb{rbk}", sinname], writes=["rb"])
            p.op("vector", lambda v, sl=sl: v.tensor_mul(
                src[:, sl], src[:, sl], cost[:, sl]),
                 reads=[srcname, cosname], writes=[srcname])
            p.op("vector", lambda v, sl=sl: v.scalar_tensor_tensor(
                dst_fn(sl), src[:, sl], 1.0, rb[:, sl], OP.mult, OP.add),
                 reads=[srcname, "rb"], writes=[dstname])

    rv = salloc("rv", [128, S], BF16, side="right")
    msq = salloc("msq", [128, S], BF16, side="right")
    qraw = [salloc(f"qraw{i}", [128, TW], BF16, side="right") for i in range(2)]
    kraw = [salloc(f"kraw{i}", [128, S], BF16, side="right") for i in range(2)]

    def rope_block(oc):
        par = oc % 2
        qdst = qp[:, oc, :]
        p.op("tensor", lambda t, par=par: t.matmul(
            pb[4][:, 0:512], pmat[:, :], qraw[par][:, 0:512],
            start=True, stop=True),
             reads=[f"qraw{par}", "pmat"], writes=["pb4"])
        p.op("tensor", lambda t, par=par: t.matmul(
            pb[5][:, 0:512], pmat[:, :], qraw[par][:, 512:1024],
            start=True, stop=True),
             reads=[f"qraw{par}", "pmat"], writes=["pb5"])
        p.op("scalar", lambda sc, qdst=qdst: sc.activation(
            qdst[:, 0:512], pb[4][:, :], AF.Identity),
             reads=["pb4"], writes=["qp"])
        p.op("scalar", lambda sc, qdst=qdst: sc.activation(
            qdst[:, 512:1024], pb[5][:, :], AF.Identity),
             reads=["pb5"], writes=["qp"])
        for tch in range(4):
            sl = slice(tch * 512, tch * 512 + 512)
            rbk = 4 + tch % 2
            p.op("tensor", lambda t, par=par, sl=sl, rbk=rbk: t.matmul(
                pb[rbk][:, :], pmat[:, :], kraw[par][:, sl],
                start=True, stop=True),
                 reads=[f"kraw{par}", "pmat"], writes=[f"pb{rbk}"])
            p.op("scalar", lambda sc, sl=sl, rbk=rbk: sc.activation(
                kpb[:, sl], pb[rbk][:, :], AF.Identity),
                 reads=[f"pb{rbk}"], writes=["kpb"])
        p.op("vector", lambda v, qdst=qdst: v.tensor_mul(
            qdst[:, :], qdst[:, :], sin_q[:, :]),
             reads=["qp", "sin_q"], writes=["qp"])
        p.op("vector", lambda v, par=par: v.tensor_mul(
            qraw[par][:, :], qraw[par][:, :], cos_q[:, :]),
             reads=[f"qraw{par}", "cos_q"], writes=[f"qraw{par}"])
        p.op("vector", lambda v, qdst=qdst, par=par: v.tensor_add(
            qdst[:, :], qdst[:, :], qraw[par][:, :]),
             reads=["qp", f"qraw{par}"], writes=["qp"])
        p.op("vector", lambda v: v.tensor_mul(
            kpb[:, :], kpb[:, :], sin_t[:, :]),
             reads=["kpb", "sin_t"], writes=["kpb"])
        p.op("vector", lambda v, par=par: v.tensor_mul(
            kraw[par][:, :], kraw[par][:, :], cos_t[:, :]),
             reads=[f"kraw{par}", "cos_t"], writes=[f"kraw{par}"])
        p.op("vector", lambda v, par=par: v.tensor_add(
            kpb[:, :], kpb[:, :], kraw[par][:, :]),
             reads=["kpb", f"kraw{par}"], writes=["kpb"])
        p.op("vector", lambda v, qdst=qdst, oc=oc: v.scalar_tensor_tensor(
            msq[:, 0:TW], qdst[:, :], 1.0, qdst[:, :], OP.mult, OP.mult,
            accum_out=msc[:, oc:oc + 1]),
             reads=["qp"], writes=["msq", "msc"])
        p.op("vector", lambda v, oc=oc: v.scalar_tensor_tensor(
            msq[:, :], kpb[:, :], 1.0, kpb[:, :], OP.mult, OP.mult,
            accum_out=msc[:, OC + oc:OC + oc + 1]),
             reads=["kpb"], writes=["msq", "msc"])
        for kt in range(NT):
            p.op("sync", lambda s, oc=oc, kt=kt: s.dma_start(
                out=kp_dram[oc, kt], in_=kpb[:, kt * 128:(kt + 1) * 128]),
                 reads=["kpb"], writes=["kp_dram"], sem="dio", inck=16)

    wload(wvs, "wvs", wv_b[0])
    wload(wqs[0], "wqs0", wq_b[0])
    wload(wks[0], "wks0", wk_b[0])
    for oc in range(OC):
        for dc in range(DC):
            for tch in range(4):
                sl = slice(tch * 512, tch * 512 + 512)
                p.op("tensor", lambda t, dc=dc, tch=tch, sl=sl:
                     t.matmul(pb[tch][:, :], wvs[:, dc, :], xn_fm[:, dc, sl],
                              start=(dc == 0), stop=(dc == DC - 1)),
                     reads=["wvs", "xn_fm"], writes=[f"pb{tch}"])
        if oc > 0:
            rope_block(oc - 1)
        if oc + 1 < OC:
            wload(wvs, "wvs", wv_b[oc + 1])
        for dc in range(DC):
            for tch in range(2):
                sl = slice(tch * 512, tch * 512 + 512)
                p.op("tensor", lambda t, dc=dc, tch=tch, sl=sl:
                     t.matmul(pb[4 + tch][:, :], wqs[0][:, dc, :],
                              xn_ffn[:, dc, sl],
                              start=(dc == 0), stop=(dc == DC - 1)),
                     reads=["wqs0", "xn_ffn"], writes=[f"pb{4 + tch}"])
        if oc + 1 < OC:
            wload(wqs[0], "wqs0", wq_b[oc + 1])
        for tch in range(4):
            sl = slice(tch * 512, tch * 512 + 512)
            p.op("scalar", lambda sc, tch=tch, sl=sl, oc=oc: sc.activation(
                rv[:, sl], pb[tch][:, :], AF.Tanh,
                bias=bvh[:, oc:oc + 1], scale=0.5),
                 reads=[f"pb{tch}", "bvh"], writes=["rv"])
            p.op("vector", lambda v, sl=sl: v.tensor_scalar(
                rv[:, sl], rv[:, sl], 0.5, 0.5, OP.mult, OP.add),
                 reads=["rv"], writes=["rv"])
        for dc in range(DC):
            for tch in range(4):
                sl = slice(tch * 512, tch * 512 + 512)
                p.op("tensor", lambda t, dc=dc, tch=tch, sl=sl:
                     t.matmul(pb[tch][:, :], wks[0][:, dc, :],
                              xn_fm[:, dc, sl],
                              start=(dc == 0), stop=(dc == DC - 1)),
                     reads=["wks0", "xn_fm"], writes=[f"pb{tch}"])
        if oc + 1 < OC:
            wload(wks[0], "wks0", wk_b[oc + 1])
        for tch in range(2):
            sl = slice(tch * 512, tch * 512 + 512)
            p.op("scalar", lambda sc, tch=tch, sl=sl, oc=oc: sc.activation(
                qraw[oc % 2][:, sl], pb[4 + tch][:, :], AF.Identity,
                bias=bqs[:, oc:oc + 1], scale=1.0),
                 reads=[f"pb{4 + tch}", "bqs"], writes=[f"qraw{oc % 2}"])
        for half in range(2):
            for j in range(8):
                kt = half * 8 + j
                p.op("tensor", lambda t, kt=kt, j=j: t.transpose(
                    pT[:, j * 128:(j + 1) * 128], rv[:, kt * 128:(kt + 1) * 128],
                    i128[:, :]),
                     reads=["rv", "i128"], writes=["pT"])
            p.op("vector", lambda v: v.tensor_copy(
                svq[:, :, :].rearrange("p kt (blk o) -> p kt blk o", o=65)
                [:, :, :, 0:64],
                pT[:, :].rearrange("p (kt blk o) -> p kt blk o", o=64, blk=2)),
                 reads=["pT"], writes=["svq"])
            for j in range(8):
                kt = half * 8 + j
                p.op("sync", lambda s, oc=oc, kt=kt, j=j: s.dma_start(
                    out=svt_dram[oc, kt], in_=svq[:, j, :]),
                     reads=["svq", "svqo"], writes=["svt_dram"], sem="dio",
                     inck=16)
        for tch in range(4):
            sl = slice(tch * 512, tch * 512 + 512)
            p.op("scalar", lambda sc, tch=tch, sl=sl, oc=oc: sc.activation(
                kraw[oc % 2][:, sl], pb[tch][:, :], AF.Identity,
                bias=bks[:, oc:oc + 1], scale=1.0),
                 reads=[f"pb{tch}", "bks"], writes=[f"kraw{oc % 2}"])
    rope_block(OC - 1)

    p.op("vector", lambda v: v.tensor_copy(mscb[:, :], msc[:, :]),
         reads=["msc"], writes=["mscb"])
    p.op("tensor", lambda t: t.matmul(pS[0:1, 0:32], ones_col[:, :], mscb[:, :],
                                      start=True, stop=True),
         reads=["mscb", "ones_col"], writes=["pS"])
    p.op("vector", lambda v: v.tensor_copy(mrow[0:1, 0:32], pS[0:1, 0:32]),
         reads=["pS"], writes=["mrow"])
    p.op("vector", lambda v: v.scalar_tensor_tensor(
        mrow[0:1, 32:48], mrow[0:1, 0:16], 2.0, mrow[0:1, 16:32],
        OP.mult, OP.add),
         reads=["mrow"], writes=["mrow2"])
    p.op("tensor", lambda t: t.matmul(pS[:, 0:16], ones_row_f[0:1, :],
                                      mrow[0:1, 32:48], start=True, stop=True),
         reads=["mrow2", "ones_row_f"], writes=["pS"])
    p.op("vector", lambda v: v.tensor_scalar(mcols[:, :], pS[:, 0:16],
                                             -1.0 / 32768.0, None, OP.mult),
         reads=["pS"], writes=["mcols"])

    sfree("kraw1", "kraw0", "qraw1", "qraw0", "msq", "rv", "svq", "kpb",
          "rb", "ra", "wvs", "wks0", "wqs0", "xn_fm")
    yb = salloc("yb", [128, OC, TW], BF16, side="right")
    kps = [salloc(f"kps{i}", [128, NT, 128], BF16, side="right")
           for i in range(2)]
    svs = [salloc(f"svs{i}", [128, NT, 130], BF16, side="right")
           for i in range(2)]
    for hp in range(2):
        for kt in range(NT):
            p.op("sync", lambda s, hp=hp, kt=kt: s.dma_start(
                out=kps[hp][:, kt, :], in_=kp_dram[hp, kt]),
                 writes=[f"kps{hp}"], sem="dio", inck=16)
            p.op("sync", lambda s, hp=hp, kt=kt: s.dma_start(
                out=svs[hp][:, kt, :], in_=svt_dram[hp, kt]),
                 writes=[f"svs{hp}x"], sem="dio", inck=16)
    p.barrier()

    sfree("pS", "pT", "pb5", "pb4", "pb3", "pb2", "pb1", "pb0")
    sc = [palloc(f"sc{i}", [128, 1024], F32) for i in range(2)]
    yB = [palloc(f"yB{i}", [128, 512], F32) for i in range(4)]

    expb = [salloc(f"expb{i}", [128, 1024], BF16, side="right")
            for i in range(3)]
    ybig = [salloc(f"ybig{i}", [128, 4, 512], BF16, side="right")
            for i in range(2)]
    swl = [salloc(f"swl{i}", [128, D], BF16, side="right") for i in range(2)]
    xnl = [salloc(f"xnl{i}", [128, D], BF16, side="right") for i in range(2)]
    swp = salloc("swp", [128, D], BF16, side="right")
    racc = salloc("racc", [128, 512], F32, side="right")
    raccb = salloc("raccb", [128, 512], BF16, side="right")
    mrow2 = salloc("mrow2", [1, 512], F32, side="right")

    rt_iter = [0]

    def router_chunk():
        it = rt_iter[0]
        if it >= EN * NT * 4:
            return
        rt_iter[0] += 1
        pc = it % 4
        st, e = (it // 4) // EN, (it // 4) % EN
        rbi = (it // 4) % 2
        xbi = (st % 2)
        if e == 0 and pc == 0:
            p.op("sync", lambda s, st=st, xbi=xbi: s.dma_start(
                out=xnl[xbi][:, :], in_=xt_dram[st]),
                 reads=["xt_dram"], writes=[f"xnl{xbi}"], sem="dio", inck=16)
        if pc == 0:
            p.op("gpsimd", lambda g, e=e, st=st, rbi=rbi: g.dma_start(
                out=swl[rbi][:, :], in_=sw_d[e, st]),
                 writes=[f"swl{rbi}"], sem="dw", inck=16)
        sl = slice(pc * 512, pc * 512 + 512)
        p.op("vector", lambda v, rbi=rbi, xbi=xbi, it=it, sl=sl:
             v.scalar_tensor_tensor(
                 swp[:, sl], xnl[xbi][:, sl], 1.0, swl[rbi][:, sl],
                 OP.mult, OP.mult, accum_out=racc[:, it:it + 1]),
             reads=[f"swl{rbi}", f"xnl{xbi}"], writes=["swp", "racc"])

    scnt = [0]
    rrf = salloc("rrf", [1, 512], F32, side="right")
    pend_norm = []

    def emit_scores(hp, bi, kt, h2):
        wstart = (kt // 2) * 128
        hbase = h2 * 64
        sb = sc[scnt[0] % 2]
        sn = f"sc{scnt[0] % 2}"
        scnt[0] += 1
        if wstart < 512:
            mms = [(wstart, 512 - wstart), (512, 512)]
        else:
            mms = [(wstart, TW - wstart)]
        for (qoff, n) in mms:
            p.op("tensor", lambda t, hp=hp, bi=bi, kt=kt, qoff=qoff,
                 n=n, sb=sb, hbase=hbase: t.matmul(
                     sb[:, qoff:qoff + n],
                     kps[bi][hbase:hbase + 64, kt, :],
                     qp[hbase:hbase + 64, hp, qoff:qoff + n],
                     start=True, stop=True),
                 reads=[f"kps{bi}", "qp"], writes=[sn])
        return sb, sn

    def emit_expav(hp, bi, kt, h2, sb, sn, ecnt):
        wstart = (kt // 2) * 128
        ntot = TW - wstart
        eb = expb[ecnt % 3]
        ebn = f"expb{ecnt % 3}"
        p.op("scalar", lambda sc_, hp=hp, wstart=wstart, ntot=ntot,
             sb=sb, eb=eb: sc_.activation(
                 eb[:, 0:ntot], sb[:, wstart:TW], AF.Exp,
                 bias=mcols[:, hp:hp + 1], scale=0.125),
             reads=[sn, "mcols"], writes=[ebn])
        p.op("vector", lambda v, kt=kt, eb=eb: v.tensor_mul(
            eb[:, 0:128], eb[:, 0:128], am[:, kt % 2, :]),
             reads=[ebn, "am"], writes=[ebn])
        if wstart < 512:
            p.op("tensor", lambda t, bi=bi, kt=kt, h2=h2,
                 wstart=wstart, eb=eb: t.matmul(
                     yB[2 * h2][0:65, wstart:512],
                     svs[bi][:, kt, h2 * 65:h2 * 65 + 65],
                     eb[:, 0:512 - wstart],
                     start=(kt == 0), stop=(kt == 7)),
                 reads=[f"svs{bi}x", ebn], writes=[f"yB{2 * h2}"])
            p.op("tensor", lambda t, bi=bi, kt=kt, h2=h2,
                 wstart=wstart, ntot=ntot, eb=eb: t.matmul(
                     yB[2 * h2 + 1][0:65, :],
                     svs[bi][:, kt, h2 * 65:h2 * 65 + 65],
                     eb[:, 512 - wstart:ntot],
                     start=(kt == 0), stop=False),
                 reads=[f"svs{bi}x", ebn], writes=[f"yB{2 * h2 + 1}"])
        else:
            p.op("tensor", lambda t, bi=bi, kt=kt, h2=h2,
                 wstart=wstart, ntot=ntot, eb=eb: t.matmul(
                     yB[2 * h2 + 1][0:65, wstart - 512:512],
                     svs[bi][:, kt, h2 * 65:h2 * 65 + 65],
                     eb[:, 0:ntot],
                     start=False, stop=(kt == NT - 1)),
                 reads=[f"svs{bi}x", ebn], writes=[f"yB{2 * h2 + 1}"])

    def norm_group(hp, h2, ci, ybi):
        def emit(bb, bbn):
            slot = 2 * h2 + ci
            hbase = h2 * 64
            p.op("scalar", lambda sc_, slot=slot, ybi=ybi: sc_.activation(
                rrf[0:1, :], ybig[ybi][64:65, slot, :], AF.Ln),
                 reads=[f"ybig{ybi}"], writes=["rrf"])
            p.op("scalar", lambda sc_: sc_.activation(
                rr[0:1, :], rrf[0:1, :], AF.Exp, bias=0.0, scale=-1.0),
                 reads=["rrf"], writes=["rr"])
            p.op("tensor", lambda t, bb=bb: t.matmul(
                bb[0:64, 0:512], ones_row[0:1, 0:64], rr[0:1, :],
                start=True, stop=True),
                 reads=["rr", "ones_row"], writes=[bbn])
            p.op("vector", lambda v, hp=hp, hbase=hbase, ci=ci, slot=slot,
                 bb=bb, ybi=ybi: v.tensor_mul(
                     yb[hbase:hbase + 64, hp, ci * 512:ci * 512 + 512],
                     ybig[ybi][0:64, slot, :], bb[0:64, 0:512]),
                 reads=[f"ybig{ybi}", bbn], writes=["yb"])
        return emit

    for hp in range(OC):
        bi = hp % 2
        if hp >= 2:
            for kt in range(NT):
                p.op("sync", lambda s, hp=hp, bi=bi, kt=kt: s.dma_start(
                    out=kps[bi][:, kt, :], in_=kp_dram[hp, kt]),
                     writes=[f"kps{bi}"], sem="dio", inck=16)
                p.op("sync", lambda s, hp=hp, bi=bi, kt=kt: s.dma_start(
                    out=svs[bi][:, kt, :], in_=svt_dram[hp, kt]),
                     writes=[f"svs{bi}x"], sem="dio", inck=16)
        units = [(kt, h2) for kt in range(NT) for h2 in range(2)]
        inflight = [emit_scores(hp, bi, *units[0]),
                    emit_scores(hp, bi, *units[1])]
        for i, u in enumerate(units):
            sb, sn = inflight[i]
            emit_expav(hp, bi, u[0], u[1], sb, sn, i)
            if i % 4 == 3 and pend_norm:
                pend_norm.pop(0)(sb, sn)
            if i + 2 < len(units):
                inflight.append(emit_scores(hp, bi, *units[i + 2]))
            router_chunk()
        ybi = hp % 2
        for slot in range(4):
            p.op("scalar", lambda sc_, slot=slot, ybi=ybi: sc_.activation(
                ybig[ybi][0:65, slot, :], yB[slot][0:65, :], AF.Identity),
                 reads=[f"yB{slot}"], writes=[f"ybig{ybi}"])
        for h2 in range(2):
            for ci in range(2):
                pend_norm.append(norm_group(hp, h2, ci, ybi))
    while pend_norm:
        pend_norm.pop(0)(sc[0], "sc0")
        if pend_norm:
            pend_norm.pop(0)(sc[1], "sc1")
    p.barrier()

    sfree("yB3", "yB2", "yB1", "yB0", "sc1", "sc0")
    pbn = {i: palloc(f"pbx{i}", [128, 512], F32) for i in (0, 1, 4, 5)}
    pTn = palloc("pTx", [128, 1024], BF16)
    pSn = palloc("pSx", [128, 512], F32)

    p.op("vector", lambda v: v.tensor_copy(raccb[:, :], racc[:, :]),
         reads=["racc"], writes=["raccb"])
    p.op("tensor", lambda t: t.matmul(pSn[0:1, 0:512], ones_col[:, :],
                                      raccb[:, :], start=True, stop=True),
         reads=["raccb", "ones_col"], writes=["pS"])
    p.op("vector", lambda v: v.tensor_copy(mrow2[0:1, 0:512], pSn[0:1, 0:512]),
         reads=["pS"], writes=["mrowx"])
    p.op("vector", lambda v: v.tensor_add(
        mrow2[0:1, 0:256].rearrange("p (g x) -> p g x", x=2),
        mrow2[0:1, 0:512].rearrange("p (g x) -> p g x", x=4)[:, :, 0:2],
        mrow2[0:1, 0:512].rearrange("p (g x) -> p g x", x=4)[:, :, 2:4]),
         reads=["mrowx"], writes=["mrowx"])
    p.op("vector", lambda v: v.tensor_add(
        mrow2[0:1, 0:128].rearrange("p (g x) -> p g x", x=1),
        mrow2[0:1, 0:256].rearrange("p (g x) -> p g x", x=2)[:, :, 0:1],
        mrow2[0:1, 0:256].rearrange("p (g x) -> p g x", x=2)[:, :, 1:2]),
         reads=["mrowx"], writes=["mrowx"])
    p.op("vector", lambda v: v.tensor_add(mrow2[0:1, 0:64], mrow2[0:1, 0:64],
                                          mrow2[0:1, 64:128]),
         reads=["mrowx"], writes=["mrowx"])
    p.op("vector", lambda v: v.tensor_add(mrow2[0:1, 0:32], mrow2[0:1, 0:32],
                                          mrow2[0:1, 32:64]),
         reads=["mrowx"], writes=["mrowx"])
    p.op("vector", lambda v: v.tensor_add(mrow2[0:1, 0:16], mrow2[0:1, 0:16],
                                          mrow2[0:1, 16:32]),
         reads=["mrowx"], writes=["mrowx"])
    p.op("vector", lambda v: v.tensor_add(lgrow[0:1, :], mrow2[0:1, 0:8],
                                          mrow2[0:1, 8:16]),
         reads=["mrowx"], writes=["lgrow"])

    if DBG:
        dbq = salloc("dbq", [128, TW], BF16, side="right")
        for oc in range(OC):
            p.op("vector", lambda v, oc=oc: v.tensor_copy(dbq[:, :],
                                                          qp[:, oc, :]),
                 reads=["qp"], writes=["dbq"])
            p.op("gpsimd", lambda g, oc=oc: g.dma_start(out=dbg_qp[oc],
                                                        in_=dbq[:, :]),
                 reads=["dbq"], writes=["dbgo"], sem="dw", inck=16)
        sfree("dbq")
    sfree("rrf", "mrow2", "raccb", "racc", "swp", "xnl1", "xnl0", "swl1",
          "swl0", "ybig1", "ybig0", "expb2", "expb1", "expb0", "svs1",
          "svs0", "kps1", "kps0")
    p.barrier()
    sfree("qp")

    s_sb = salloc("s_sb", [128, DC, TW], BF16, side="right")
    wcb = [salloc(f"wcb{i}", [128, OC, 128], BF16, side="right")
           for i in range(2)]
    for doc in range(DC):
        wi = doc % 2
        p.op("gpsimd", lambda g, doc=doc, wi=wi: g.dma_start(
            out=wcb[wi][:, :, :], in_=wo_b[doc]),
             writes=[f"wcb{wi}"], sem="dw", inck=16)
        for tch in range(2):
            sl = slice(tch * 512, tch * 512 + 512)
            bk = tch % 2
            for oc in range(OC):
                p.op("tensor", lambda t, oc=oc, sl=sl, bk=bk, wi=wi: t.matmul(
                    pbn[bk][:, :], wcb[wi][:, oc, :],
                    yb[:, oc, sl], start=(oc == 0), stop=(oc == OC - 1)),
                     reads=[f"wcb{wi}", "yb"], writes=[f"pb{bk}"])
            p.op("scalar", lambda sc, doc=doc, sl=sl, bk=bk: sc.activation(
                s_sb[:, doc, sl], pbn[bk][:, :], AF.Identity,
                bias=bos[:, doc:doc + 1], scale=1.0),
                 reads=[f"pb{bk}", "bos"], writes=["s_sb"])
    p.barrier()
    sfree("wcb1", "wcb0")

    p.op("vector", lambda v: v.tensor_add(lgrow[0:1, :], lgrow[0:1, :],
                                          swbs[0:1, :]),
         reads=["lgrow", "swbs"], writes=["lgrow"])
    p.op("vector", lambda v: v.reduce_max(psc2[0:1, 0:1], lgrow[0:1, :], AX.X),
         reads=["lgrow"], writes=["psc2"])
    p.op("vector", lambda v: v.tensor_scalar(probs[0:1, :], lgrow[0:1, :],
                                             psc2[0:1, 0:1], None, OP.subtract),
         reads=["lgrow", "psc2"], writes=["probs"])
    p.op("scalar", lambda sc: sc.activation(probs[0:1, :], probs[0:1, :],
                                            AF.Exp),
         reads=["probs"], writes=["probs"])
    p.op("vector", lambda v: v.reduce_sum(psc2[0:1, 1:2], probs[0:1, :], AX.X),
         reads=["probs"], writes=["psc2b"])
    p.op("vector", lambda v: v.reciprocal(psc2[0:1, 2:3], psc2[0:1, 1:2]),
         reads=["psc2b"], writes=["psc2c"])
    p.op("vector", lambda v: v.tensor_scalar(probs[0:1, :], probs[0:1, :],
                                             psc2[0:1, 2:3], None, OP.mult),
         reads=["probs", "psc2c"], writes=["probs"])
    p.op("vector", lambda v: v.reduce_max(psc2[0:1, 0:1], probs[0:1, :], AX.X),
         reads=["probs"], writes=["psc2"])
    p.op("vector", lambda v: v.tensor_scalar(pscr[0:1, :], probs[0:1, :],
                                             psc2[0:1, 0:1], None, OP.is_ge),
         reads=["probs", "psc2"], writes=["pscr"])
    p.op("vector", lambda v: v.scalar_tensor_tensor(gates[0:1, :], pscr[0:1, :],
                                                    -BIG, probs[0:1, :],
                                                    OP.mult, OP.add),
         reads=["pscr", "probs"], writes=["gates"])
    p.op("vector", lambda v: v.reduce_max(psc2[0:1, 1:2], gates[0:1, :], AX.X),
         reads=["gates"], writes=["psc2b"])
    p.op("vector", lambda v: v.tensor_scalar(gates[0:1, :], gates[0:1, :],
                                             psc2[0:1, 1:2], None, OP.is_ge),
         reads=["gates", "psc2b"], writes=["gates"])
    p.op("vector", lambda v: v.tensor_add(pscr[0:1, :], pscr[0:1, :],
                                          gates[0:1, :]),
         reads=["pscr", "gates"], writes=["pscr"])
    p.op("vector", lambda v: v.tensor_mul(gates[0:1, :], probs[0:1, :],
                                          pscr[0:1, :]),
         reads=["probs", "pscr"], writes=["gates"])
    p.op("tensor", lambda t: t.matmul(pSn[:, 0:8], ones_row_f[0:1, :],
                                      gates[0:1, :], start=True, stop=True),
         reads=["gates", "ones_row_f"], writes=["pS"])
    p.op("vector", lambda v: v.tensor_copy(gbc[:, :], pSn[:, 0:8]),
         reads=["pS"], writes=["gbc"])
    p.op("tensor", lambda t: t.matmul(pSn[0:8, 8:9], gates[0:1, :],
                                      ones_row_f[0:1, 0:1], start=True,
                                      stop=True),
         reads=["gates", "ones_row_f"], writes=["pS"])
    p.op("vector", lambda v: v.tensor_copy(gcol[:, :], pSn[0:8, 8:9]),
         reads=["pS"], writes=["gcol"])
    p.op("tensor", lambda t: t.matmul(pSn[:, 9:10], b2ts[:, :], gcol[:, :],
                                      start=True, stop=True),
         reads=["b2ts", "gcol"], writes=["pS"])
    p.op("vector", lambda v: v.tensor_copy(b2g[:, :], pSn[:, 9:10]),
         reads=["pS"], writes=["b2g"])

    u_sb = salloc("u_sb", [128, DC, TW], BF16, side="right")
    w1s = salloc("w1s", [128, EN, 128], BF16, side="right")
    w2s = salloc("w2s", [128, EN, 128], BF16, side="right")
    ghs = salloc("ghs", [128, EN, 512], BF16, side="right")
    p.op("gpsimd", lambda g: g.dma_start(out=w1s[:, :, :], in_=w1t_d),
         writes=["w1s"], sem="dw", inck=16)
    p.op("gpsimd", lambda g: g.dma_start(out=w2s[:, :, :], in_=w2t_d),
         writes=["w2s"], sem="dw", inck=16)
    for e in range(EN):
        p.op("vector", lambda v, e=e: v.tensor_scalar(
            w2s[:, e, :], w2s[:, e, :], gbc[:, e:e + 1], None, OP.mult),
             reads=["w2s", "gbc"], writes=["w2s"])

    for hh in range(DC):
        for tch in range(2):
            sl = slice(tch * 512, tch * 512 + 512)
            hb = tch % 2
            ob = 4 + (tch % 2)
            for e in range(EN):
                p.op("tensor", lambda t, hh=hh, e=e, sl=sl, hb=hb: t.matmul(
                    pbn[hb][:, :], w1s[:, e, :], xn_ffn[:, hh, sl],
                    start=True, stop=True),
                     reads=["w1s", "xn_ffn"], writes=[f"pb{hb}"])
                p.op("scalar", lambda sc, e=e, hb=hb: sc.activation(
                    ghs[:, e, :], pbn[hb][:, :], AF.Gelu,
                    bias=b1s[:, e:e + 1], scale=1.0),
                     reads=[f"pb{hb}", "b1s"], writes=["ghs"])
            for e in range(EN):
                p.op("tensor", lambda t, e=e, ob=ob: t.matmul(
                    pbn[ob][:, :], w2s[:, e, :], ghs[:, e, :],
                    start=(e == 0), stop=(e == EN - 1)),
                     reads=["w2s", "ghs"], writes=[f"pb{ob}"])
            p.op("vector", lambda v, hh=hh, sl=sl, ob=ob: v.scalar_tensor_tensor(
                u_sb[:, hh, sl], pbn[ob][:, :], b2g[:, :], xn_ffn[:, hh, sl],
                OP.add, OP.add),
                 reads=[f"pb{ob}", "b2g", "xn_ffn"], writes=["u_sb"])
    p.barrier()
    sfree("ghs", "w2s", "w1s")
    sfree("xn_ffn")

    g_sb = salloc("g_sb", [128, DC, TW], BF16, side="right")
    w3cb = [salloc(f"w3cb{i}", [128, DC, 128], BF16, side="right")
            for i in range(2)]
    for doc in range(DC):
        wi = doc % 2
        p.op("gpsimd", lambda g, doc=doc, wi=wi: g.dma_start(
            out=w3cb[wi][:, :, :], in_=w3t_d[doc]),
             writes=[f"w3cb{wi}"], sem="dw", inck=16)
        for tch in range(2):
            sl = slice(tch * 512, tch * 512 + 512)
            bk = tch % 2
            for dc in range(DC):
                p.op("tensor", lambda t, dc=dc, sl=sl, bk=bk, wi=wi: t.matmul(
                    pbn[bk][:, :], w3cb[wi][:, dc, :],
                    u_sb[:, dc, sl], start=(dc == 0), stop=(dc == DC - 1)),
                     reads=[f"w3cb{wi}", "u_sb"], writes=[f"pb{bk}"])
            p.op("scalar", lambda sc, doc=doc, sl=sl, bk=bk: sc.activation(
                g_sb[:, doc, sl], pbn[bk][:, :], AF.Silu,
                bias=b3s[:, doc:doc + 1], scale=1.0),
                 reads=[f"pb{bk}", "b3s"], writes=["g_sb"])
    p.barrier()
    sfree("w3cb1", "w3cb0")

    sg_sb = salloc("sg_sb", [128, DC, TW], BF16, side="right")
    for doc in range(DC):
        p.op("vector", lambda v, doc=doc: v.tensor_mul(
            sg_sb[:, doc, :], s_sb[:, doc, :], g_sb[:, doc, :]),
             reads=["s_sb", "g_sb"], writes=["sg_sb"])
    xrb = salloc("xrb", [128, D], F32, side="right")
    otb = [salloc(f"otb{i}", [128, D], F32, side="right") for i in range(2)]
    sgt = salloc("sgt", [128, D], BF16, side="right")
    x_ffn_t = x_ffn.rearrange("(n p) d -> n p d", p=128)
    for i in range(NTH):
        bi = i % 2
        p.op("sync", lambda s, i=i: s.dma_start(out=xrb[:, :], in_=x_ffn_t[i]),
             writes=["xrb"], sem="dio", inck=16)
        for half in range(2):
            for j in range(8):
                dc = half * 8 + j
                p.op("tensor", lambda t, i=i, dc=dc, j=j: t.transpose(
                    pTn[:, j * 128:(j + 1) * 128],
                    sg_sb[:, dc, i * 128:(i + 1) * 128], i128[:, :]),
                     reads=["sg_sb", "i128"], writes=["pT"])
            p.op("vector", lambda v, half=half: v.tensor_copy(
                sgt[:, half * 1024:half * 1024 + 1024], pTn[:, :]),
                 reads=["pT"], writes=["sgt"])
        p.op("vector", lambda v, bi=bi: v.tensor_add(otb[bi][:, :], xrb[:, :],
                                                     sgt[:, :]),
             reads=["xrb", "sgt"], writes=[f"otb{bi}"])
        p.op("sync", lambda s, i=i, bi=bi: s.dma_start(out=out_d[i],
                                                       in_=otb[bi][:, :]),
             reads=[f"otb{bi}"], writes=["out_d"], sem="dio", inck=16)
    if DBG:
        dbx = salloc("dbx", [128, TW], BF16, side="right")

        def dump(dst_ap, src_ap, names):
            n = src_ap.shape[-1]
            p.op("vector", lambda v: v.tensor_copy(dbx[:, 0:n], src_ap),
                 reads=list(names), writes=["dbx"])
            p.op("gpsimd", lambda g: g.dma_start(out=dst_ap, in_=dbx[:, 0:n]),
                 reads=["dbx"], writes=["dbgo"], sem="dw", inck=16)

        for oc in range(OC):
            dump(dbg_yb[oc], yb[:, oc, :], ["yb"])
        for st in range(NT):
            p.op("gpsimd", lambda g, st=st: g.dma_start(
                out=dbg_xn[st], in_=xt_dram[st]),
                 reads=["xt_dram"], writes=["dbgo"], sem="dw", inck=16)
        for oc in range(OC):
            for kt in range(NT):
                p.op("gpsimd", lambda g, oc=oc, kt=kt: g.dma_start(
                    out=dbg_kp[oc, kt], in_=kp_dram[oc, kt]),
                     reads=["kp_dram"], writes=["dbgo"], sem="dw", inck=16)
                p.op("gpsimd", lambda g, oc=oc, kt=kt: g.dma_start(
                    out=dbg_sv[oc, kt], in_=svt_dram[oc, kt]),
                     reads=["svt_dram"], writes=["dbgo"], sem="dw", inck=16)
        for doc in range(DC):
            dump(dbg_s[doc], s_sb[:, doc, :], ["s_sb"])
            dump(dbg_g[doc], g_sb[:, doc, :], ["g_sb"])
        p.op("vector", lambda v: v.tensor_copy(dbx[:, 0:OC], mcols[:, :]),
             reads=["mcols"], writes=["dbx"])
        p.op("vector", lambda v: v.tensor_copy(dbx[:, 32:64], msc[:, :]),
             reads=["msc"], writes=["dbx"])
        p.op("vector", lambda v: v.tensor_copy(dbx[0:1, 64:72], lgrow[0:1, :]),
             reads=["lgrow"], writes=["dbx"])
        p.op("vector", lambda v: v.tensor_copy(dbx[0:1, 72:80], gates[0:1, :]),
             reads=["gates"], writes=["dbx"])
        p.op("vector", lambda v: v.tensor_copy(dbx[0:1, 80:88], probs[0:1, :]),
             reads=["probs"], writes=["dbx"])
        p.op("gpsimd", lambda g: g.dma_start(out=dbg_sc, in_=dbx[:, 0:256]),
             reads=["dbx"], writes=["dbgo"], sem="dw", inck=16)
        p.op("gpsimd", lambda g: g.dma_start(out=dbg_tab[0], in_=cos_t[:, :]),
             reads=["cos_t"], writes=["dbgo"], sem="dw", inck=16)
        p.op("gpsimd", lambda g: g.dma_start(out=dbg_tab[1], in_=sin_t[:, :]),
             reads=["sin_t"], writes=["dbgo"], sem="dw", inck=16)
        p.op("gpsimd", lambda g: g.dma_start(out=dbg_tab[2][:, 0:TW],
                                             in_=cos_q[:, :]),
             reads=["cos_q"], writes=["dbgo"], sem="dw", inck=16)
        p.op("gpsimd", lambda g: g.dma_start(out=dbg_tab[3][:, 0:TW],
                                             in_=sin_q[:, :]),
             reads=["sin_q"], writes=["dbgo"], sem="dw", inck=16)
        p.op("gpsimd", lambda g: g.dma_start(out=dbg_pm[0], in_=pmat[:, :]),
             reads=["pmat"], writes=["dbgo"], sem="dw", inck=16)
        p.op("gpsimd", lambda g: g.dma_start(out=dbg_pm[1], in_=i128[:, :]),
             reads=["i128"], writes=["dbgo"], sem="dw", inck=16)
    fin = p.counts["dio"]

    p.burst_map()
    sems = {}
    sem_cms = []
    for n in ("ph", "pe", "act", "dve", "pool", "dio", "dw"):
        cm = nc.semaphore(n)
        sems[n] = cm.__enter__()
        sem_cms.append(cm)
    with nc.allow_low_precision("bf16 intermediates are deliberate"), \
            nc.Block() as block:
        @block.sync
        def _(sync):
            p.emit("sync", sync, sems)
            sync.wait_ge(sems["dio"], p.bm.get("dio", {}).get(fin, fin))

        @block.tensor
        def _(tensor):
            p.emit("tensor", tensor, sems)

        @block.vector
        def _(vector):
            p.emit("vector", vector, sems)

        @block.scalar
        def _(scalar):
            p.emit("scalar", scalar, sems)

        @block.gpsimd
        def _(gpsimd):
            p.emit("gpsimd", gpsimd, sems)

    for name in reversed(list(live)):
        live.pop(name).__exit__(None, None, None)
    for cm in sem_cms:
        cm.__exit__(None, None, None)
    return nc


_NC_CACHE = None
_LAST_RES = None


def _get_nc():
    global _NC_CACHE
    if _NC_CACHE is None:
        _NC_CACHE = build_program()
    return _NC_CACHE


def kernel(L0_attn_x, L0_attn_Wq, L0_attn_bq, L0_attn_Wk, L0_attn_bk,
           L0_attn_Wv, L0_attn_bv, L0_attn_Wo, L0_attn_bo,
           L0_ffn_switch_w, L0_ffn_switch_b, L0_ffn_w1, L0_ffn_b1,
           L0_ffn_w2, L0_ffn_b2, L0_ffn_w3, L0_ffn_b3):
    f = np.float32
    x = np.ascontiguousarray(np.asarray(L0_attn_x, dtype=f))
    Wq = np.asarray(L0_attn_Wq, dtype=f)
    Wk = np.asarray(L0_attn_Wk, dtype=f)
    Wv = np.asarray(L0_attn_Wv, dtype=f)
    Wo = np.asarray(L0_attn_Wo, dtype=f)
    sw = np.asarray(L0_ffn_switch_w, dtype=f)
    w1 = np.asarray(L0_ffn_w1, dtype=f)
    w2 = np.asarray(L0_ffn_w2, dtype=f)
    w3 = np.asarray(L0_ffn_w3, dtype=f)

    def blkw(WT):
        return np.ascontiguousarray(
            WT.reshape(DC, 128, OC, 128).transpose(2, 1, 0, 3))

    wqb = blkw(Wq.T)
    wkb = blkw(Wk.T)
    wvb = blkw(Wv.T)
    wob = np.ascontiguousarray(
        Wo.T.reshape(OC, 128, DC, 128).transpose(2, 1, 0, 3))
    w3t = np.ascontiguousarray(
        w3.T.reshape(DC, 128, DC, 128).transpose(2, 1, 0, 3))
    w1t = np.ascontiguousarray(w1.transpose(2, 0, 1))
    w2t = np.ascontiguousarray(w2.transpose(2, 0, 1))
    swb = np.ascontiguousarray(sw.reshape(EN, NT, 128, D))
    bq_c = np.ascontiguousarray(np.asarray(L0_attn_bq, f).reshape(OC, 128).T)
    bk_c = np.ascontiguousarray(np.asarray(L0_attn_bk, f).reshape(OC, 128).T)
    bv_c = np.ascontiguousarray(np.asarray(L0_attn_bv, f).reshape(OC, 128).T)
    bo_c = np.ascontiguousarray(np.asarray(L0_attn_bo, f).reshape(DC, 128).T)
    b3_c = np.ascontiguousarray(np.asarray(L0_ffn_b3, f).reshape(DC, 128).T)
    b1_c = np.ascontiguousarray(np.asarray(L0_ffn_b1, f).T)
    b2t = np.ascontiguousarray(np.asarray(L0_ffn_b2, f))
    swb_r = np.ascontiguousarray(np.asarray(L0_ffn_switch_b, f).reshape(1, EN))
    pos = np.arange(S, dtype=f).reshape(1, S)

    in_maps = []
    for c in range(8):
        b, r = c // 2, c % 2
        own = [r + 2 * i for i in range(NTH)]
        rows = np.concatenate([np.arange(g * 128, (g + 1) * 128) for g in own])
        posq = rows.astype(f).reshape(1, TW)
        in_maps.append({
            "x_attn": x[b],
            "x_ffn": np.ascontiguousarray(x[b][rows, :]),
            "wq_b": wqb, "wk_b": wkb, "wv_b": wvb,
            "wo_b": wob, "w3t": w3t, "w1t": w1t, "w2t": w2t,
            "sw_b": swb,
            "bq_c": bq_c, "bk_c": bk_c, "bv_c": bv_c,
            "bo_c": bo_c, "b3_c": b3_c, "b1_c": b1_c, "b2t": b2t,
            "swb": swb_r, "pos": pos, "posq": posq,
            "parc": np.full((128, 1), float(r), f),
        })

    nc = _get_nc()
    res = run_bass_kernel_spmd(nc, in_maps, core_ids=list(range(8)))
    global _LAST_RES
    _LAST_RES = res
    out = np.empty((4, S, D), dtype=np.float32)
    for c in range(8):
        b, r = c // 2, c % 2
        for i in range(NTH):
            g = r + 2 * i
            out[b, g * 128:(g + 1) * 128, :] = res.results[c]["out"][i]
    return out


if __name__ == "__main__":
    build_program()
    print("build ok")


# version 24
# speedup vs baseline: 1.1425x; 1.1425x over previous
import os

import numpy as np

import concourse.bass as bass
import concourse.mybir as mybir
from concourse.bass_utils import run_bass_kernel_spmd

F32 = mybir.dt.float32
BF16 = mybir.dt.bfloat16
AF = mybir.ActivationFunctionType
OP = mybir.AluOpType
AX = mybir.AxisListType

DBG = bool(int(os.environ.get("MK_DEBUG", "0")))

D = 2048
S = 2048
NT = 16
NTH = 8
TW = 1024
OC = 16
DC = 16
EN = 8
PI = float(np.pi)
LOG1E4 = float(np.log(10000.0))
BIG = 1.0e30

ENGS = ("sync", "tensor", "vector", "scalar", "gpsimd")
ESEM = {"sync": "dio", "tensor": "pe", "vector": "dve", "scalar": "act",
        "gpsimd": "dw"}
DMASEMS = ("dio", "dw")


class Ev:
    __slots__ = ("sem", "val")

    def __init__(self, sem, val):
        self.sem = sem
        self.val = val


class Prog:

    def __init__(self):
        self.q = {e: [] for e in ENGS}
        self.counts = {"ph": 0}
        self.phase_n = 0
        self.bufw = {}
        self.bufr = {}

    def op(self, eng, fn, reads=(), writes=(), sem=None, inck=1):
        sem = sem or ESEM[eng]
        is_dma = sem in DMASEMS
        waits = []
        for b in reads:
            w = self.bufw.get(b)
            if w is not None:
                waits.append(w)
        for b in writes:
            w = self.bufw.get(b)
            if w is not None:
                waits.append(w)
            waits.extend(self.bufr.get(b, ()))
        pruned = {}
        for w in waits:
            wdma = w.sem in DMASEMS
            if w.sem == sem == "pe":
                continue
            if pruned.get(w.sem, -1) < w.val:
                pruned[w.sem] = w.val
        self.counts[sem] = self.counts.get(sem, 0) + inck
        ev = Ev(sem, self.counts[sem])
        self.q[eng].append(("op", fn, tuple(Ev(s, v) for s, v in pruned.items()),
                            sem, inck))
        for b in reads:
            self.bufr.setdefault(b, []).append(ev)
        for b in writes:
            self.bufw[b] = ev
            self.bufr[b] = []
        return ev

    def barrier(self):
        self.phase_n += 1
        self.counts["ph"] += len(ENGS)
        tgt = self.phase_n * len(ENGS)
        for e in ENGS:
            self.q[e].append(("bar", tgt))
        ev = Ev("ph", tgt)
        for b in list(self.bufw):
            self.bufw[b] = ev
        self.bufr = {}

    def burst_map(self):
        owner = {"dio": "sync", "dw": "gpsimd"}
        self.bm = {s: {} for s in owner}
        for s, eng in owner.items():
            run = []
            cum = 0
            for item in self.q[eng]:
                is_burst = (item[0] == "op" and item[4] == 16
                            and item[3] == s and not item[2])
                if item[0] == "op" and item[3] == s:
                    cum += item[4]
                if is_burst:
                    run.append(cum)
                else:
                    for v in run:
                        self.bm[s][v] = run[-1]
                    run = []
                    if item[0] == "op" and item[3] == s:
                        run = [cum]
            for v in run:
                self.bm[s][v] = run[-1]

    def emit(self, eng_name, engobj, sems):
        waited = {}
        for item in self.q[eng_name]:
            if item[0] == "bar":
                engobj.sem_inc(sems["ph"], 1)
                engobj.wait_ge(sems["ph"], item[1])
                waited = {"ph": item[1]}
                continue
            _, fn, waits, sem, inck = item
            for w in waits:
                val = self.bm.get(w.sem, {}).get(w.val, w.val)
                if waited.get(w.sem, -1) >= val:
                    continue
                engobj.wait_ge(sems[w.sem], val)
                waited[w.sem] = val
            res = fn(engobj)
            last = res[-1] if isinstance(res, (list, tuple)) else res
            last.then_inc(sems[sem], inck)


def build_program():
    nc = bass.Bass("TRN2", target_bir_lowering=False)

    def din(name, shape):
        return nc.dram_tensor(name, shape, F32, kind="ExternalInput").ap()

    x_attn = din("x_attn", [S, D])
    x_ffn = din("x_ffn", [TW, D])
    wq_b = din("wq_b", [OC, 128, DC, 128])
    wk_b = din("wk_b", [OC, 128, DC, 128])
    wv_b = din("wv_b", [OC, 128, DC, 128])
    wo_b = din("wo_b", [DC, 128, OC, 128])
    w3t_d = din("w3t", [DC, 128, DC, 128])
    w1t_d = din("w1t", [128, EN, 128])
    w2t_d = din("w2t", [128, EN, 128])
    sw_d = din("sw_b", [EN, NT, 128, D])
    bq_d = din("bq_c", [128, OC])
    bk_d = din("bk_c", [128, OC])
    bv_d = din("bv_c", [128, OC])
    bo_d = din("bo_c", [128, DC])
    b3_d = din("b3_c", [128, DC])
    b1_d = din("b1_c", [128, EN])
    b2t_d = din("b2t", [EN, 128])
    swb_d = din("swb", [1, EN])
    pos_d = din("pos", [1, S])
    posq_d = din("posq", [1, TW])
    parc_d = din("parc", [128, 1])
    out_d = nc.dram_tensor("out", [NTH, 128, D], F32, kind="ExternalOutput").ap()
    if DBG:
        dbg_qp = nc.dram_tensor("dbg_qp", [OC, 128, TW], F32,
                                kind="ExternalOutput").ap()
        dbg_kp = nc.dram_tensor("dbg_kp", [OC, NT, 128, 128], F32,
                                kind="ExternalOutput").ap()
        dbg_sv = nc.dram_tensor("dbg_sv", [OC, NT, 128, 128], F32,
                                kind="ExternalOutput").ap()
        dbg_xn = nc.dram_tensor("dbg_xn", [NT, 128, D], F32,
                                kind="ExternalOutput").ap()
        dbg_yb = nc.dram_tensor("dbg_yb", [OC, 128, TW], F32,
                                kind="ExternalOutput").ap()
        dbg_s = nc.dram_tensor("dbg_s", [DC, 128, TW], F32,
                               kind="ExternalOutput").ap()
        dbg_g = nc.dram_tensor("dbg_g", [DC, 128, TW], F32,
                               kind="ExternalOutput").ap()
        dbg_u = nc.dram_tensor("dbg_u", [DC, 128, TW], F32,
                               kind="ExternalOutput").ap()
        dbg_sc = nc.dram_tensor("dbg_sc", [128, 256], F32,
                                kind="ExternalOutput").ap()
        dbg_tab = nc.dram_tensor("dbg_tab", [4, 128, S], F32,
                                 kind="ExternalOutput").ap()
        dbg_pm = nc.dram_tensor("dbg_pm", [2, 128, 128], F32,
                                kind="ExternalOutput").ap()

    kp_dram = nc.dram_tensor("kp_dram", [OC, NT, 128, 128], BF16).ap()
    svt_dram = nc.dram_tensor("svt_dram", [OC, NT, 128, 130], BF16).ap()
    xt_dram = nc.dram_tensor("xt_dram", [NT, 128, D], BF16).ap()
    i32_dram = nc.dram_tensor("i32_dram", [32, 32], BF16).ap()

    p = Prog()
    live = {}

    def salloc(name, shape, dtype, side="left"):
        cm = nc.sbuf_tensor(name, shape, dtype, side=side)
        h = cm.__enter__()
        live[name] = cm
        return h

    def sfree(*names):
        for name in names:
            live.pop(name).__exit__(None, None, None)

    cos_t = salloc("cos_t", [128, S], BF16)
    sin_t = salloc("sin_t", [128, S], BF16)
    cos_q = salloc("cos_q", [128, TW], BF16)
    sin_q = salloc("sin_q", [128, TW], BF16)
    i128 = salloc("i128", [128, 128], BF16)
    pmat = salloc("pmat", [128, 128], BF16)
    tril = salloc("tril", [128, 128], BF16)
    am = salloc("am", [128, 2, 128], BF16)
    ones_col = salloc("ones_col", [128, 1], BF16)
    ones_row = salloc("ones_row", [1, 128], BF16)
    ones_row_f = salloc("ones_row_f", [1, 128], F32)
    negpi = salloc("negpi", [128, 1], F32)
    epsc = salloc("epsc", [128, 1], F32)
    parc = salloc("parcs", [128, 1], F32)
    bqs = salloc("bqs", [128, OC], F32)
    bks = salloc("bks", [128, OC], F32)
    bvh = salloc("bvh", [128, OC], F32)
    bos = salloc("bos", [128, DC], F32)
    b3s = salloc("b3s", [128, DC], F32)
    b1s = salloc("b1s", [128, EN], F32)
    b2ts = salloc("b2ts", [EN, 128], F32)
    swbs = salloc("swbs", [1, EN], F32)

    mcols = salloc("mcols", [128, OC], F32)
    lnsc = salloc("lnsc", [128, 12], F32)
    msc = salloc("msc", [128, 32], F32)
    mscb = salloc("mscb", [128, 32], BF16)
    mrow = salloc("mrow", [1, 64], F32)
    lgrow = salloc("lgrow", [1, EN], F32)
    probs = salloc("probs", [1, EN], F32)
    pscr = salloc("pscr", [1, EN], F32)
    psc2 = salloc("psc2", [1, 4], F32)
    gates = salloc("gates", [1, EN], F32)
    gbc = salloc("gbc", [128, EN], F32)
    gcol = salloc("gcol", [EN, 1], F32)
    b2g = salloc("b2g", [128, 1], F32)
    rr = salloc("rr", [1, 512], BF16)

    xn_ffn = salloc("xn_ffn", [128, DC, TW], BF16)
    qp = salloc("qp", [128, OC, TW], BF16)

    def palloc(name, shape, dtype):
        cm = nc.psum_tensor(name, shape, dtype)
        h = cm.__enter__()
        live[name] = cm
        return h

    pb = [palloc(f"pb{i}", [128, 512], F32) for i in range(6)]
    pT = palloc("pT", [128, 1024], BF16)
    pS = palloc("pS", [128, 512], F32)

    p.op("gpsimd", lambda g: g.iota(i128[:, :], [[1, 128]], channel_multiplier=-1,
                                    base=0, allow_small_or_imprecise_dtypes=True),
         writes=["i128"], sem="pool")
    p.op("vector", lambda v: v.tensor_single_scalar(i128[:, :], i128[:, :], 0.0,
                                                    OP.is_equal),
         reads=["i128"], writes=["i128"])
    p.op("gpsimd", lambda g: g.iota(tril[:, :], [[1, 128]], channel_multiplier=-1,
                                    base=0, allow_small_or_imprecise_dtypes=True),
         writes=["tril"], sem="pool")
    p.op("vector", lambda v: v.tensor_single_scalar(tril[:, :], tril[:, :], 0.0,
                                                    OP.is_ge),
         reads=["tril"], writes=["tril"])
    p.op("vector", lambda v: v.memset(ones_col[:, :], 1.0), writes=["ones_col"])
    p.op("vector", lambda v: v.memset(ones_row[:, :], 1.0), writes=["ones_row"])
    p.op("vector", lambda v: v.memset(ones_row_f[:, :], 1.0),
         writes=["ones_row_f"])
    p.op("vector", lambda v: v.memset(negpi[:, :], -PI), writes=["negpi"])
    p.op("vector", lambda v: v.memset(epsc[:, :], 1e-5), writes=["epsc"])

    def ld(dstname, dst, src):
        return p.op("sync", lambda s: s.dma_start(out=dst, in_=src),
                    writes=[dstname], sem="dio", inck=16)

    ld("bqs", bqs[:, :], bq_d)
    ld("bks", bks[:, :], bk_d)
    ld("bvh", bvh[:, :], bv_d)
    ld("bos", bos[:, :], bo_d)
    ld("b3s", b3s[:, :], b3_d)
    ld("b1s", b1s[:, :], b1_d)
    ld("b2ts", b2ts[:, :], b2t_d)
    ld("swbs", swbs[:, :], swb_d)
    ld("parcs", parc[:, :], parc_d)
    p.op("vector", lambda v: v.tensor_single_scalar(bvh[:, :], bvh[:, :], 0.5,
                                                    OP.mult),
         reads=["bvh"], writes=["bvh"])
    p.op("vector", lambda v: v.tensor_scalar(am[:, 0, :], tril[:, :],
                                             parc[:, 0:1], None, OP.max),
         reads=["tril", "parcs"], writes=["am"])
    p.op("vector", lambda v: v.tensor_scalar(am[:, 1, :], tril[:, :],
                                             parc[:, 0:1], None, OP.min),
         reads=["tril", "parcs"], writes=["am"])

    p.op("vector", lambda v: v.memset(pmat[:, :], 0.0), writes=["pmat"])
    p.op("sync", lambda s: s.dma_start(out=i32_dram, in_=i128[0:32, 0:32]),
         reads=["i128"], writes=["i32d"], sem="dio", inck=16)
    for (pr, pc) in ((0, 32), (32, 0), (64, 96), (96, 64)):
        p.op("sync",
             lambda s, pr=pr, pc=pc: s.dma_start(
                 out=pmat[pr:pr + 32, pc:pc + 32], in_=i32_dram),
             reads=["i32d"], writes=["pmat"], sem="dio", inck=16)

    poss = salloc("poss", [1, S], F32, side="right")
    possq = salloc("possq", [1, TW], F32, side="right")
    ld("poss", poss[:, :], pos_d)
    ld("possq", possq[:, :], posq_d)
    invf = salloc("invf", [1, 128], F32, side="right")
    sgn = salloc("sgn", [128, 1], F32, side="right")
    sgr = salloc("sgr", [1, 128], F32, side="right")
    ang = salloc("ang", [128, S], F32, side="right")
    aki = salloc("aki", [128, 512], mybir.dt.int32, side="right")
    akf = salloc("akf", [128, 512], F32, side="right")
    akm = salloc("akm", [128, 512], F32, side="right")
    p.op("gpsimd", lambda g: g.iota(invf[:, :], [[0, 4], [1, 32]],
                                    channel_multiplier=0, base=0,
                                    allow_small_or_imprecise_dtypes=True),
         writes=["invf"], sem="pool")
    p.op("scalar", lambda sc: sc.activation(invf[:, :], invf[:, :], AF.Exp,
                                            bias=0.0, scale=-LOG1E4 / 32.0),
         reads=["invf"], writes=["invf"])
    p.op("gpsimd", lambda g: g.iota(sgr[:, :], [[0, 2], [1, 64]],
                                    channel_multiplier=0, base=0,
                                    allow_small_or_imprecise_dtypes=True),
         writes=["sgr"], sem="pool")
    p.op("vector", lambda v: v.tensor_single_scalar(sgr[:, :], sgr[:, :], 32.0,
                                                    OP.is_lt),
         reads=["sgr"], writes=["sgr"])
    p.op("vector", lambda v: v.tensor_scalar(sgr[:, :], sgr[:, :], -2.0, 1.0,
                                             OP.mult, OP.add),
         reads=["sgr"], writes=["sgr"])
    p.op("tensor", lambda t: t.matmul(pS[:, 0:1], sgr[0:1, :],
                                      ones_row_f[0:1, 0:1], start=True,
                                      stop=True),
         reads=["sgr", "ones_row_f"], writes=["pS"])
    p.op("vector", lambda v: v.tensor_copy(sgn[:, :], pS[:, 0:1]),
         reads=["pS"], writes=["sgn"])

    def angles(shift, posrow, posname, nw, dstname, dst):
        for ch in range(nw):
            sl = slice(ch * 512, ch * 512 + 512)
            p.op("tensor",
                 lambda t, sl=sl, ch=ch: t.matmul(pb[ch % 4][:, :], invf[0:1, :],
                                                  posrow[0:1, sl], start=True,
                                                  stop=True),
                 reads=["invf", posname], writes=[f"pb{ch % 4}"])
            p.op("vector", lambda v, ch=ch: v.tensor_scalar(
                akf[:, :], pb[ch % 4][:, :], shift, 1.0 / (2 * PI), OP.add,
                OP.mult),
                 reads=[f"pb{ch % 4}"], writes=["akf"])
            p.op("vector", lambda v: v.tensor_copy(aki[:, :], akf[:, :]),
                 reads=["akf"], writes=["aki"])
            p.op("vector", lambda v: v.tensor_copy(akf[:, :], aki[:, :]),
                 reads=["aki"], writes=["akf"])
            p.op("vector", lambda v, ch=ch: v.scalar_tensor_tensor(
                akf[:, :], akf[:, :], -2 * PI, pb[ch % 4][:, :], OP.mult,
                OP.add),
                 reads=["akf", f"pb{ch % 4}"], writes=["akf"])
            p.op("vector", lambda v: v.tensor_single_scalar(
                akf[:, :], akf[:, :], shift, OP.add),
                 reads=["akf"], writes=["akf"])
            p.op("vector", lambda v: v.tensor_single_scalar(
                akm[:, :], akf[:, :], PI, OP.is_gt),
                 reads=["akf"], writes=["akm"])
            p.op("vector", lambda v: v.scalar_tensor_tensor(
                akf[:, :], akm[:, :], -2 * PI, akf[:, :], OP.mult, OP.add),
                 reads=["akm", "akf"], writes=["akf"])
            p.op("vector", lambda v: v.tensor_single_scalar(
                akm[:, :], akf[:, :], -PI, OP.is_lt),
                 reads=["akf"], writes=["akm"])
            p.op("vector", lambda v, sl=sl: v.scalar_tensor_tensor(
                ang[:, sl], akm[:, :], 2 * PI, akf[:, :], OP.mult, OP.add),
                 reads=["akm", "akf"], writes=["ang"])
        p.op("scalar", lambda sc, nw=nw: sc.activation(
            dst[:, 0:nw * 512], ang[:, 0:nw * 512], AF.Sin, bias=0.0,
            scale=1.0),
             reads=["ang"], writes=[dstname])

    angles(0.0, poss, "poss", 4, "sin_t", sin_t)
    angles(PI / 2, poss, "poss", 4, "cos_t", cos_t)
    angles(0.0, possq, "possq", 2, "sin_q", sin_q)
    angles(PI / 2, possq, "possq", 2, "cos_q", cos_q)
    p.op("vector", lambda v: v.tensor_scalar(sin_t[:, :], sin_t[:, :],
                                             sgn[:, 0:1], None, OP.mult),
         reads=["sin_t", "sgn"], writes=["sin_t"])
    p.op("vector", lambda v: v.tensor_scalar(sin_q[:, :], sin_q[:, :],
                                             sgn[:, 0:1], None, OP.mult),
         reads=["sin_q", "sgn"], writes=["sin_q"])
    sfree("akm", "akf", "aki", "ang", "sgr", "sgn", "invf",
          "possq", "poss")
    p.barrier()

    xn_fm = salloc("xn_fm", [128, DC, S], BF16, side="right")
    xt = [salloc(f"xt{i}", [128, D], F32, side="right") for i in range(2)]
    xnt = [salloc(f"xnt{i}", [128, D], BF16, side="right") for i in range(2)]
    xsq = salloc("xsq", [128, D], BF16, side="right")

    def emit_ln(dstname, dst_fm, src_tiled, ntiles, to_dram):
        def transpose_tile(i):
            bi = i % 2
            for half in range(2):
                for j in range(8):
                    dc = half * 8 + j
                    p.op("tensor", lambda t, dc=dc, j=j, bi=bi: t.transpose(
                        pT[:, j * 128:(j + 1) * 128],
                        xnt[bi][:, dc * 128:(dc + 1) * 128], i128[:, :]),
                         reads=[f"xnt{bi}", "i128"], writes=["pT"])
                p.op("vector", lambda v, i=i, half=half: v.tensor_copy(
                    dst_fm[:, half * 8:half * 8 + 8, i * 128:(i + 1) * 128],
                    pT[:, :].rearrange("p (dc t) -> p dc t", t=128)),
                     reads=["pT"], writes=[dstname])

        for i in range(ntiles):
            bi = i % 2
            bn = f"xt{bi}"
            c0 = 6 * bi
            p.op("sync", lambda s, i=i, bi=bi: s.dma_start(out=xt[bi][:, :],
                                                           in_=src_tiled[i]),
                 writes=[bn], sem="dio", inck=16)
            p.op("vector", lambda v, bi=bi, c0=c0: v.reduce_sum(
                lnsc[:, c0:c0 + 1], xt[bi][:, :], AX.X),
                 reads=[bn], writes=[f"ls{bi}"])
            p.op("scalar", lambda sc, bi=bi, c0=c0: sc.activation(
                xsq[:, :], xt[bi][:, :], AF.Square,
                accum_out=lnsc[:, c0 + 1:c0 + 2]),
                 reads=[bn], writes=["xsq", f"lq{bi}"])
            p.op("vector", lambda v, c0=c0: v.tensor_single_scalar(
                lnsc[:, c0:c0 + 1], lnsc[:, c0:c0 + 1], 1.0 / D, OP.mult),
                 reads=[f"ls{bi}"], writes=[f"ls{bi}"])
            p.op("vector", lambda v, c0=c0: v.tensor_mul(
                lnsc[:, c0 + 2:c0 + 3], lnsc[:, c0:c0 + 1],
                lnsc[:, c0:c0 + 1]),
                 reads=[f"ls{bi}"], writes=[f"lm{bi}"])
            p.op("vector", lambda v, c0=c0: v.scalar_tensor_tensor(
                lnsc[:, c0 + 3:c0 + 4], lnsc[:, c0 + 1:c0 + 2], 1.0 / D,
                lnsc[:, c0 + 2:c0 + 3], OP.mult, OP.subtract),
                 reads=[f"lq{bi}", f"lm{bi}"], writes=[f"lv{bi}"])
            p.op("scalar", lambda sc, c0=c0: sc.activation(
                lnsc[:, c0 + 4:c0 + 5], lnsc[:, c0 + 3:c0 + 4], AF.Sqrt,
                bias=epsc[:, 0:1], scale=1.0),
                 reads=[f"lv{bi}", "epsc"], writes=[f"lsd{bi}"])
            p.op("vector", lambda v, c0=c0: v.reciprocal(
                lnsc[:, c0 + 5:c0 + 6], lnsc[:, c0 + 4:c0 + 5]),
                 reads=[f"lsd{bi}"], writes=[f"lr{bi}"])
            p.op("vector", lambda v, bi=bi, c0=c0: v.tensor_scalar(
                xnt[bi][:, :], xt[bi][:, :], lnsc[:, c0:c0 + 1],
                lnsc[:, c0 + 5:c0 + 6], OP.subtract, OP.mult),
                 reads=[bn, f"ls{bi}", f"lr{bi}"], writes=[f"xnt{bi}"])
            if to_dram:
                p.op("sync", lambda s, i=i, bi=bi: s.dma_start(
                    out=xt_dram[i], in_=xnt[bi][:, :]),
                     reads=[f"xnt{bi}"], writes=["xt_dram"], sem="dio",
                     inck=16)
            if i > 0:
                transpose_tile(i - 1)
        transpose_tile(ntiles - 1)

    emit_ln("xn_fm", xn_fm, x_attn.rearrange("(n p) d -> n p d", p=128), NT,
            True)
    emit_ln("xn_ffn", xn_ffn, x_ffn.rearrange("(n p) d -> n p d", p=128), NTH,
            False)
    sfree("xsq", "xnt1", "xnt0", "xt1", "xt0")
    p.barrier()

    wqs1 = salloc("wqs0", [128, DC, 128], BF16, side="right")
    wks1 = salloc("wks0", [128, DC, 128], BF16, side="right")
    wvs = salloc("wvs", [128, DC, 128], BF16, side="right")
    wqs = [wqs1, wqs1]
    wks = [wks1, wks1]
    ra = salloc("ra", [128, S], BF16, side="right")
    rb = salloc("rb", [128, S], BF16, side="right")
    kpb = salloc("kpb", [128, S], BF16, side="right")
    svq = salloc("svq", [128, 8, 130], BF16, side="right")

    def wload(dst, dstname, src):
        p.op("gpsimd", lambda g: g.dma_start(out=dst[:, :, :], in_=src),
             writes=[dstname], sem="dw", inck=16)

    p.op("vector", lambda v: v.memset(svq[:, :, 64:65], 1.0), writes=["svqo"])
    p.op("vector", lambda v: v.memset(svq[:, :, 129:130], 1.0), writes=["svqo"])

    def rope(src, srcname, cosname, cost, sinname, sint, dst_fn, dstname, nw):
        for tch in range(nw):
            sl = slice(tch * 512, tch * 512 + 512)
            rbk = 4 + tch % 2
            p.op("tensor", lambda t, sl=sl, rbk=rbk: t.matmul(
                pb[rbk][:, :], pmat[:, :], src[:, sl], start=True, stop=True),
                 reads=[srcname, "pmat"], writes=[f"pb{rbk}"])
            p.op("vector", lambda v, sl=sl, rbk=rbk: v.tensor_mul(
                rb[:, sl], pb[rbk][:, :], sint[:, sl]),
                 reads=[f"pb{rbk}", sinname], writes=["rb"])
            p.op("vector", lambda v, sl=sl: v.tensor_mul(
                src[:, sl], src[:, sl], cost[:, sl]),
                 reads=[srcname, cosname], writes=[srcname])
            p.op("vector", lambda v, sl=sl: v.scalar_tensor_tensor(
                dst_fn(sl), src[:, sl], 1.0, rb[:, sl], OP.mult, OP.add),
                 reads=[srcname, "rb"], writes=[dstname])

    rv = salloc("rv", [128, S], BF16, side="right")
    msq = salloc("msq", [128, S], BF16, side="right")
    qraw = [salloc(f"qraw{i}", [128, TW], BF16, side="right") for i in range(2)]
    kraw = [salloc(f"kraw{i}", [128, S], BF16, side="right") for i in range(2)]

    def rope_block(oc):
        par = oc % 2
        qdst = qp[:, oc, :]
        p.op("tensor", lambda t, par=par: t.matmul(
            pb[4][:, 0:512], pmat[:, :], qraw[par][:, 0:512],
            start=True, stop=True),
             reads=[f"qraw{par}", "pmat"], writes=["pb4"])
        p.op("tensor", lambda t, par=par: t.matmul(
            pb[5][:, 0:512], pmat[:, :], qraw[par][:, 512:1024],
            start=True, stop=True),
             reads=[f"qraw{par}", "pmat"], writes=["pb5"])
        p.op("scalar", lambda sc, qdst=qdst: sc.activation(
            qdst[:, 0:512], pb[4][:, :], AF.Identity),
             reads=["pb4"], writes=["qp"])
        p.op("scalar", lambda sc, qdst=qdst: sc.activation(
            qdst[:, 512:1024], pb[5][:, :], AF.Identity),
             reads=["pb5"], writes=["qp"])
        for tch in range(4):
            sl = slice(tch * 512, tch * 512 + 512)
            rbk = 4 + tch % 2
            p.op("tensor", lambda t, par=par, sl=sl, rbk=rbk: t.matmul(
                pb[rbk][:, :], pmat[:, :], kraw[par][:, sl],
                start=True, stop=True),
                 reads=[f"kraw{par}", "pmat"], writes=[f"pb{rbk}"])
            p.op("scalar", lambda sc, sl=sl, rbk=rbk: sc.activation(
                kpb[:, sl], pb[rbk][:, :], AF.Identity),
                 reads=[f"pb{rbk}"], writes=["kpb"])
        p.op("vector", lambda v, qdst=qdst: v.tensor_mul(
            qdst[:, :], qdst[:, :], sin_q[:, :]),
             reads=["qp", "sin_q"], writes=["qp"])
        p.op("vector", lambda v, par=par: v.tensor_mul(
            qraw[par][:, :], qraw[par][:, :], cos_q[:, :]),
             reads=[f"qraw{par}", "cos_q"], writes=[f"qraw{par}"])
        p.op("vector", lambda v, qdst=qdst, par=par: v.tensor_add(
            qdst[:, :], qdst[:, :], qraw[par][:, :]),
             reads=["qp", f"qraw{par}"], writes=["qp"])
        p.op("vector", lambda v: v.tensor_mul(
            kpb[:, :], kpb[:, :], sin_t[:, :]),
             reads=["kpb", "sin_t"], writes=["kpb"])
        p.op("vector", lambda v, par=par: v.tensor_mul(
            kraw[par][:, :], kraw[par][:, :], cos_t[:, :]),
             reads=[f"kraw{par}", "cos_t"], writes=[f"kraw{par}"])
        p.op("vector", lambda v, par=par: v.tensor_add(
            kpb[:, :], kpb[:, :], kraw[par][:, :]),
             reads=["kpb", f"kraw{par}"], writes=["kpb"])
        p.op("vector", lambda v, qdst=qdst, oc=oc: v.scalar_tensor_tensor(
            msq[:, 0:TW], qdst[:, :], 1.0, qdst[:, :], OP.mult, OP.mult,
            accum_out=msc[:, oc:oc + 1]),
             reads=["qp"], writes=["msq", "msc"])
        p.op("vector", lambda v, oc=oc: v.scalar_tensor_tensor(
            msq[:, :], kpb[:, :], 1.0, kpb[:, :], OP.mult, OP.mult,
            accum_out=msc[:, OC + oc:OC + oc + 1]),
             reads=["kpb"], writes=["msq", "msc"])
        for kt in range(NT):
            p.op("sync", lambda s, oc=oc, kt=kt: s.dma_start(
                out=kp_dram[oc, kt], in_=kpb[:, kt * 128:(kt + 1) * 128]),
                 reads=["kpb"], writes=["kp_dram"], sem="dio", inck=16)

    wload(wvs, "wvs", wv_b[0])
    wload(wqs[0], "wqs0", wq_b[0])
    wload(wks[0], "wks0", wk_b[0])
    for oc in range(OC):
        for dc in range(DC):
            for tch in range(4):
                sl = slice(tch * 512, tch * 512 + 512)
                p.op("tensor", lambda t, dc=dc, tch=tch, sl=sl:
                     t.matmul(pb[tch][:, :], wvs[:, dc, :], xn_fm[:, dc, sl],
                              start=(dc == 0), stop=(dc == DC - 1)),
                     reads=["wvs", "xn_fm"], writes=[f"pb{tch}"])
        if oc > 0:
            rope_block(oc - 1)
        if oc + 1 < OC:
            wload(wvs, "wvs", wv_b[oc + 1])
        for dc in range(DC):
            for tch in range(2):
                sl = slice(tch * 512, tch * 512 + 512)
                p.op("tensor", lambda t, dc=dc, tch=tch, sl=sl:
                     t.matmul(pb[4 + tch][:, :], wqs[0][:, dc, :],
                              xn_ffn[:, dc, sl],
                              start=(dc == 0), stop=(dc == DC - 1)),
                     reads=["wqs0", "xn_ffn"], writes=[f"pb{4 + tch}"])
        if oc + 1 < OC:
            wload(wqs[0], "wqs0", wq_b[oc + 1])
        for tch in range(4):
            sl = slice(tch * 512, tch * 512 + 512)
            p.op("scalar", lambda sc, tch=tch, sl=sl, oc=oc: sc.activation(
                rv[:, sl], pb[tch][:, :], AF.Tanh,
                bias=bvh[:, oc:oc + 1], scale=0.5),
                 reads=[f"pb{tch}", "bvh"], writes=["rv"])
            p.op("vector", lambda v, sl=sl: v.tensor_scalar(
                rv[:, sl], rv[:, sl], 0.5, 0.5, OP.mult, OP.add),
                 reads=["rv"], writes=["rv"])
        for dc in range(DC):
            for tch in range(4):
                sl = slice(tch * 512, tch * 512 + 512)
                p.op("tensor", lambda t, dc=dc, tch=tch, sl=sl:
                     t.matmul(pb[tch][:, :], wks[0][:, dc, :],
                              xn_fm[:, dc, sl],
                              start=(dc == 0), stop=(dc == DC - 1)),
                     reads=["wks0", "xn_fm"], writes=[f"pb{tch}"])
        if oc + 1 < OC:
            wload(wks[0], "wks0", wk_b[oc + 1])
        for tch in range(2):
            sl = slice(tch * 512, tch * 512 + 512)
            p.op("scalar", lambda sc, tch=tch, sl=sl, oc=oc: sc.activation(
                qraw[oc % 2][:, sl], pb[4 + tch][:, :], AF.Identity,
                bias=bqs[:, oc:oc + 1], scale=1.0),
                 reads=[f"pb{4 + tch}", "bqs"], writes=[f"qraw{oc % 2}"])
        for half in range(2):
            for j in range(8):
                kt = half * 8 + j
                p.op("tensor", lambda t, kt=kt, j=j: t.transpose(
                    pT[:, j * 128:(j + 1) * 128], rv[:, kt * 128:(kt + 1) * 128],
                    i128[:, :]),
                     reads=["rv", "i128"], writes=["pT"])
            p.op("vector", lambda v: v.tensor_copy(
                svq[:, :, :].rearrange("p kt (blk o) -> p kt blk o", o=65)
                [:, :, :, 0:64],
                pT[:, :].rearrange("p (kt blk o) -> p kt blk o", o=64, blk=2)),
                 reads=["pT"], writes=["svq"])
            for j in range(8):
                kt = half * 8 + j
                p.op("sync", lambda s, oc=oc, kt=kt, j=j: s.dma_start(
                    out=svt_dram[oc, kt], in_=svq[:, j, :]),
                     reads=["svq", "svqo"], writes=["svt_dram"], sem="dio",
                     inck=16)
        for tch in range(4):
            sl = slice(tch * 512, tch * 512 + 512)
            p.op("scalar", lambda sc, tch=tch, sl=sl, oc=oc: sc.activation(
                kraw[oc % 2][:, sl], pb[tch][:, :], AF.Identity,
                bias=bks[:, oc:oc + 1], scale=1.0),
                 reads=[f"pb{tch}", "bks"], writes=[f"kraw{oc % 2}"])
    rope_block(OC - 1)

    p.op("vector", lambda v: v.tensor_copy(mscb[:, :], msc[:, :]),
         reads=["msc"], writes=["mscb"])
    p.op("tensor", lambda t: t.matmul(pS[0:1, 0:32], ones_col[:, :], mscb[:, :],
                                      start=True, stop=True),
         reads=["mscb", "ones_col"], writes=["pS"])
    p.op("vector", lambda v: v.tensor_copy(mrow[0:1, 0:32], pS[0:1, 0:32]),
         reads=["pS"], writes=["mrow"])
    p.op("vector", lambda v: v.scalar_tensor_tensor(
        mrow[0:1, 32:48], mrow[0:1, 0:16], 2.0, mrow[0:1, 16:32],
        OP.mult, OP.add),
         reads=["mrow"], writes=["mrow2"])
    p.op("tensor", lambda t: t.matmul(pS[:, 0:16], ones_row_f[0:1, :],
                                      mrow[0:1, 32:48], start=True, stop=True),
         reads=["mrow2", "ones_row_f"], writes=["pS"])
    p.op("vector", lambda v: v.tensor_scalar(mcols[:, :], pS[:, 0:16],
                                             -1.0 / 32768.0, None, OP.mult),
         reads=["pS"], writes=["mcols"])

    sfree("kraw1", "kraw0", "qraw1", "qraw0", "msq", "rv", "svq", "kpb",
          "rb", "ra", "wvs", "wks0", "wqs0", "xn_fm")
    yb = salloc("yb", [128, OC, TW], BF16, side="right")
    kps = [salloc(f"kps{i}", [128, NT, 128], BF16, side="right")
           for i in range(2)]
    svs = [salloc(f"svs{i}", [128, NT, 130], BF16, side="right")
           for i in range(2)]
    for hp in range(2):
        for kt in range(NT):
            p.op("sync", lambda s, hp=hp, kt=kt: s.dma_start(
                out=kps[hp][:, kt, :], in_=kp_dram[hp, kt]),
                 writes=[f"kps{hp}"], sem="dio", inck=16)
            p.op("sync", lambda s, hp=hp, kt=kt: s.dma_start(
                out=svs[hp][:, kt, :], in_=svt_dram[hp, kt]),
                 writes=[f"svs{hp}x"], sem="dio", inck=16)
    p.barrier()

    sfree("pS", "pT", "pb5", "pb4", "pb3", "pb2", "pb1", "pb0")
    sc = [palloc(f"sc{i}", [128, 1024], F32) for i in range(2)]
    yB = [palloc(f"yB{i}", [128, 512], F32) for i in range(4)]

    expb = [salloc(f"expb{i}", [128, 1024], BF16, side="right")
            for i in range(3)]
    ybig = [salloc(f"ybig{i}", [128, 4, 512], BF16, side="right")
            for i in range(2)]
    swl = [salloc(f"swl{i}", [128, D], BF16, side="right") for i in range(2)]
    xnl = [salloc(f"xnl{i}", [128, D], BF16, side="right") for i in range(2)]
    swp = salloc("swp", [128, D], BF16, side="right")
    racc = salloc("racc", [128, 512], F32, side="right")
    raccb = salloc("raccb", [128, 512], BF16, side="right")
    mrow2 = salloc("mrow2", [1, 512], F32, side="right")

    rt_iter = [0]

    def router_chunk():
        it = rt_iter[0]
        if it >= EN * NT * 4:
            return
        rt_iter[0] += 1
        pc = it % 4
        st, e = (it // 4) // EN, (it // 4) % EN
        rbi = (it // 4) % 2
        xbi = (st % 2)
        if e == 0 and pc == 0:
            p.op("sync", lambda s, st=st, xbi=xbi: s.dma_start(
                out=xnl[xbi][:, :], in_=xt_dram[st]),
                 reads=["xt_dram"], writes=[f"xnl{xbi}"], sem="dio", inck=16)
        if pc == 0:
            p.op("gpsimd", lambda g, e=e, st=st, rbi=rbi: g.dma_start(
                out=swl[rbi][:, :], in_=sw_d[e, st]),
                 writes=[f"swl{rbi}"], sem="dw", inck=16)
        sl = slice(pc * 512, pc * 512 + 512)
        p.op("vector", lambda v, rbi=rbi, xbi=xbi, it=it, sl=sl:
             v.scalar_tensor_tensor(
                 swp[:, sl], xnl[xbi][:, sl], 1.0, swl[rbi][:, sl],
                 OP.mult, OP.mult, accum_out=racc[:, it:it + 1]),
             reads=[f"swl{rbi}", f"xnl{xbi}"], writes=["swp", "racc"])

    scnt = [0]
    rrf = salloc("rrf", [1, 512], F32, side="right")
    pend_norm = []

    def emit_scores(hp, bi, kt, h2):
        wstart = (kt // 2) * 128
        hbase = h2 * 64
        sb = sc[scnt[0] % 2]
        sn = f"sc{scnt[0] % 2}"
        scnt[0] += 1
        if wstart < 512:
            mms = [(wstart, 512 - wstart), (512, 512)]
        else:
            mms = [(wstart, TW - wstart)]
        for (qoff, n) in mms:
            p.op("tensor", lambda t, hp=hp, bi=bi, kt=kt, qoff=qoff,
                 n=n, sb=sb, hbase=hbase: t.matmul(
                     sb[:, qoff:qoff + n],
                     kps[bi][hbase:hbase + 64, kt, :],
                     qp[hbase:hbase + 64, hp, qoff:qoff + n],
                     start=True, stop=True),
                 reads=[f"kps{bi}", "qp"], writes=[sn])
        return sb, sn

    def emit_expav(hp, bi, kt, h2, sb, sn, ecnt):
        wstart = (kt // 2) * 128
        ntot = TW - wstart
        eb = expb[ecnt % 3]
        ebn = f"expb{ecnt % 3}"
        p.op("scalar", lambda sc_, hp=hp, wstart=wstart, ntot=ntot,
             sb=sb, eb=eb: sc_.activation(
                 eb[:, 0:ntot], sb[:, wstart:TW], AF.Exp,
                 bias=mcols[:, hp:hp + 1], scale=0.125),
             reads=[sn, "mcols"], writes=[ebn])
        p.op("vector", lambda v, kt=kt, eb=eb: v.tensor_mul(
            eb[:, 0:128], eb[:, 0:128], am[:, kt % 2, :]),
             reads=[ebn, "am"], writes=[ebn])
        if wstart < 512:
            p.op("tensor", lambda t, bi=bi, kt=kt, h2=h2,
                 wstart=wstart, eb=eb: t.matmul(
                     yB[2 * h2][0:65, wstart:512],
                     svs[bi][:, kt, h2 * 65:h2 * 65 + 65],
                     eb[:, 0:512 - wstart],
                     start=(kt == 0), stop=(kt == 7)),
                 reads=[f"svs{bi}x", ebn], writes=[f"yB{2 * h2}"])
            p.op("tensor", lambda t, bi=bi, kt=kt, h2=h2,
                 wstart=wstart, ntot=ntot, eb=eb: t.matmul(
                     yB[2 * h2 + 1][0:65, :],
                     svs[bi][:, kt, h2 * 65:h2 * 65 + 65],
                     eb[:, 512 - wstart:ntot],
                     start=(kt == 0), stop=False),
                 reads=[f"svs{bi}x", ebn], writes=[f"yB{2 * h2 + 1}"])
        else:
            p.op("tensor", lambda t, bi=bi, kt=kt, h2=h2,
                 wstart=wstart, ntot=ntot, eb=eb: t.matmul(
                     yB[2 * h2 + 1][0:65, wstart - 512:512],
                     svs[bi][:, kt, h2 * 65:h2 * 65 + 65],
                     eb[:, 0:ntot],
                     start=False, stop=(kt == NT - 1)),
                 reads=[f"svs{bi}x", ebn], writes=[f"yB{2 * h2 + 1}"])

    def norm_group(hp, h2, ci, ybi):
        def emit(bb, bbn):
            slot = 2 * h2 + ci
            hbase = h2 * 64
            p.op("scalar", lambda sc_, slot=slot, ybi=ybi: sc_.activation(
                rrf[0:1, :], ybig[ybi][64:65, slot, :], AF.Ln),
                 reads=[f"ybig{ybi}"], writes=["rrf"])
            p.op("scalar", lambda sc_: sc_.activation(
                rr[0:1, :], rrf[0:1, :], AF.Exp, bias=0.0, scale=-1.0),
                 reads=["rrf"], writes=["rr"])
            p.op("tensor", lambda t, bb=bb: t.matmul(
                bb[0:64, 0:512], ones_row[0:1, 0:64], rr[0:1, :],
                start=True, stop=True),
                 reads=["rr", "ones_row"], writes=[bbn])
            p.op("vector", lambda v, hp=hp, hbase=hbase, ci=ci, slot=slot,
                 bb=bb, ybi=ybi: v.tensor_mul(
                     yb[hbase:hbase + 64, hp, ci * 512:ci * 512 + 512],
                     ybig[ybi][0:64, slot, :], bb[0:64, 0:512]),
                 reads=[f"ybig{ybi}", bbn], writes=["yb"])
        return emit

    for hp in range(OC):
        bi = hp % 2
        if hp >= 2:
            for kt in range(NT):
                p.op("sync", lambda s, hp=hp, bi=bi, kt=kt: s.dma_start(
                    out=kps[bi][:, kt, :], in_=kp_dram[hp, kt]),
                     writes=[f"kps{bi}"], sem="dio", inck=16)
                p.op("sync", lambda s, hp=hp, bi=bi, kt=kt: s.dma_start(
                    out=svs[bi][:, kt, :], in_=svt_dram[hp, kt]),
                     writes=[f"svs{bi}x"], sem="dio", inck=16)
        units = [(kt, h2) for kt in range(NT) for h2 in range(2)]
        inflight = [emit_scores(hp, bi, *units[0]),
                    emit_scores(hp, bi, *units[1])]
        for i, u in enumerate(units):
            sb, sn = inflight[i]
            emit_expav(hp, bi, u[0], u[1], sb, sn, i)
            if i % 4 == 3 and pend_norm:
                pend_norm.pop(0)(sb, sn)
            if i + 2 < len(units):
                inflight.append(emit_scores(hp, bi, *units[i + 2]))
            router_chunk()
        ybi = hp % 2
        for slot in range(4):
            p.op("scalar", lambda sc_, slot=slot, ybi=ybi: sc_.activation(
                ybig[ybi][0:65, slot, :], yB[slot][0:65, :], AF.Identity),
                 reads=[f"yB{slot}"], writes=[f"ybig{ybi}"])
        for h2 in range(2):
            for ci in range(2):
                pend_norm.append(norm_group(hp, h2, ci, ybi))
    while pend_norm:
        pend_norm.pop(0)(sc[0], "sc0")
        if pend_norm:
            pend_norm.pop(0)(sc[1], "sc1")
    p.barrier()

    sfree("yB3", "yB2", "yB1", "yB0", "sc1", "sc0")
    pbn = {i: palloc(f"pbx{i}", [128, 512], F32) for i in (0, 1, 4, 5)}
    pTn = palloc("pTx", [128, 1024], BF16)
    pSn = palloc("pSx", [128, 512], F32)

    p.op("vector", lambda v: v.tensor_copy(raccb[:, :], racc[:, :]),
         reads=["racc"], writes=["raccb"])
    p.op("tensor", lambda t: t.matmul(pSn[0:1, 0:512], ones_col[:, :],
                                      raccb[:, :], start=True, stop=True),
         reads=["raccb", "ones_col"], writes=["pS"])
    p.op("vector", lambda v: v.tensor_copy(mrow2[0:1, 0:512], pSn[0:1, 0:512]),
         reads=["pS"], writes=["mrowx"])
    p.op("vector", lambda v: v.tensor_add(
        mrow2[0:1, 0:256].rearrange("p (g x) -> p g x", x=2),
        mrow2[0:1, 0:512].rearrange("p (g x) -> p g x", x=4)[:, :, 0:2],
        mrow2[0:1, 0:512].rearrange("p (g x) -> p g x", x=4)[:, :, 2:4]),
         reads=["mrowx"], writes=["mrowx"])
    p.op("vector", lambda v: v.tensor_add(
        mrow2[0:1, 0:128].rearrange("p (g x) -> p g x", x=1),
        mrow2[0:1, 0:256].rearrange("p (g x) -> p g x", x=2)[:, :, 0:1],
        mrow2[0:1, 0:256].rearrange("p (g x) -> p g x", x=2)[:, :, 1:2]),
         reads=["mrowx"], writes=["mrowx"])
    p.op("vector", lambda v: v.tensor_add(mrow2[0:1, 0:64], mrow2[0:1, 0:64],
                                          mrow2[0:1, 64:128]),
         reads=["mrowx"], writes=["mrowx"])
    p.op("vector", lambda v: v.tensor_add(mrow2[0:1, 0:32], mrow2[0:1, 0:32],
                                          mrow2[0:1, 32:64]),
         reads=["mrowx"], writes=["mrowx"])
    p.op("vector", lambda v: v.tensor_add(mrow2[0:1, 0:16], mrow2[0:1, 0:16],
                                          mrow2[0:1, 16:32]),
         reads=["mrowx"], writes=["mrowx"])
    p.op("vector", lambda v: v.tensor_add(lgrow[0:1, :], mrow2[0:1, 0:8],
                                          mrow2[0:1, 8:16]),
         reads=["mrowx"], writes=["lgrow"])

    if DBG:
        dbq = salloc("dbq", [128, TW], BF16, side="right")
        for oc in range(OC):
            p.op("vector", lambda v, oc=oc: v.tensor_copy(dbq[:, :],
                                                          qp[:, oc, :]),
                 reads=["qp"], writes=["dbq"])
            p.op("gpsimd", lambda g, oc=oc: g.dma_start(out=dbg_qp[oc],
                                                        in_=dbq[:, :]),
                 reads=["dbq"], writes=["dbgo"], sem="dw", inck=16)
        sfree("dbq")
    sfree("rrf", "mrow2", "raccb", "racc", "swp", "xnl1", "xnl0", "swl1",
          "swl0", "ybig1", "ybig0", "expb2", "expb1", "expb0", "svs1",
          "svs0", "kps1", "kps0")
    p.barrier()
    sfree("qp")

    s_sb = salloc("s_sb", [128, DC, TW], BF16, side="right")
    wcb = [salloc(f"wcb{i}", [128, OC, 128], BF16, side="right")
           for i in range(2)]
    for doc in range(DC):
        wi = doc % 2
        p.op("gpsimd", lambda g, doc=doc, wi=wi: g.dma_start(
            out=wcb[wi][:, :, :], in_=wo_b[doc]),
             writes=[f"wcb{wi}"], sem="dw", inck=16)
        for tch in range(2):
            sl = slice(tch * 512, tch * 512 + 512)
            bk = tch % 2
            for oc in range(OC):
                p.op("tensor", lambda t, oc=oc, sl=sl, bk=bk, wi=wi: t.matmul(
                    pbn[bk][:, :], wcb[wi][:, oc, :],
                    yb[:, oc, sl], start=(oc == 0), stop=(oc == OC - 1)),
                     reads=[f"wcb{wi}", "yb"], writes=[f"pb{bk}"])
            p.op("scalar", lambda sc, doc=doc, sl=sl, bk=bk: sc.activation(
                s_sb[:, doc, sl], pbn[bk][:, :], AF.Identity,
                bias=bos[:, doc:doc + 1], scale=1.0),
                 reads=[f"pb{bk}", "bos"], writes=["s_sb"])
    p.barrier()
    sfree("wcb1", "wcb0")

    p.op("vector", lambda v: v.tensor_add(lgrow[0:1, :], lgrow[0:1, :],
                                          swbs[0:1, :]),
         reads=["lgrow", "swbs"], writes=["lgrow"])
    p.op("vector", lambda v: v.reduce_max(psc2[0:1, 0:1], lgrow[0:1, :], AX.X),
         reads=["lgrow"], writes=["psc2"])
    p.op("vector", lambda v: v.tensor_scalar(probs[0:1, :], lgrow[0:1, :],
                                             psc2[0:1, 0:1], None, OP.subtract),
         reads=["lgrow", "psc2"], writes=["probs"])
    p.op("scalar", lambda sc: sc.activation(probs[0:1, :], probs[0:1, :],
                                            AF.Exp),
         reads=["probs"], writes=["probs"])
    p.op("vector", lambda v: v.reduce_sum(psc2[0:1, 1:2], probs[0:1, :], AX.X),
         reads=["probs"], writes=["psc2b"])
    p.op("vector", lambda v: v.reciprocal(psc2[0:1, 2:3], psc2[0:1, 1:2]),
         reads=["psc2b"], writes=["psc2c"])
    p.op("vector", lambda v: v.tensor_scalar(probs[0:1, :], probs[0:1, :],
                                             psc2[0:1, 2:3], None, OP.mult),
         reads=["probs", "psc2c"], writes=["probs"])
    p.op("vector", lambda v: v.reduce_max(psc2[0:1, 0:1], probs[0:1, :], AX.X),
         reads=["probs"], writes=["psc2"])
    p.op("vector", lambda v: v.tensor_scalar(pscr[0:1, :], probs[0:1, :],
                                             psc2[0:1, 0:1], None, OP.is_ge),
         reads=["probs", "psc2"], writes=["pscr"])
    p.op("vector", lambda v: v.scalar_tensor_tensor(gates[0:1, :], pscr[0:1, :],
                                                    -BIG, probs[0:1, :],
                                                    OP.mult, OP.add),
         reads=["pscr", "probs"], writes=["gates"])
    p.op("vector", lambda v: v.reduce_max(psc2[0:1, 1:2], gates[0:1, :], AX.X),
         reads=["gates"], writes=["psc2b"])
    p.op("vector", lambda v: v.tensor_scalar(gates[0:1, :], gates[0:1, :],
                                             psc2[0:1, 1:2], None, OP.is_ge),
         reads=["gates", "psc2b"], writes=["gates"])
    p.op("vector", lambda v: v.tensor_add(pscr[0:1, :], pscr[0:1, :],
                                          gates[0:1, :]),
         reads=["pscr", "gates"], writes=["pscr"])
    p.op("vector", lambda v: v.tensor_mul(gates[0:1, :], probs[0:1, :],
                                          pscr[0:1, :]),
         reads=["probs", "pscr"], writes=["gates"])
    p.op("tensor", lambda t: t.matmul(pSn[:, 0:8], ones_row_f[0:1, :],
                                      gates[0:1, :], start=True, stop=True),
         reads=["gates", "ones_row_f"], writes=["pS"])
    p.op("vector", lambda v: v.tensor_copy(gbc[:, :], pSn[:, 0:8]),
         reads=["pS"], writes=["gbc"])
    p.op("tensor", lambda t: t.matmul(pSn[0:8, 8:9], gates[0:1, :],
                                      ones_row_f[0:1, 0:1], start=True,
                                      stop=True),
         reads=["gates", "ones_row_f"], writes=["pS"])
    p.op("vector", lambda v: v.tensor_copy(gcol[:, :], pSn[0:8, 8:9]),
         reads=["pS"], writes=["gcol"])
    p.op("tensor", lambda t: t.matmul(pSn[:, 9:10], b2ts[:, :], gcol[:, :],
                                      start=True, stop=True),
         reads=["b2ts", "gcol"], writes=["pS"])
    p.op("vector", lambda v: v.tensor_copy(b2g[:, :], pSn[:, 9:10]),
         reads=["pS"], writes=["b2g"])

    u_sb = salloc("u_sb", [128, DC, TW], BF16, side="right")
    w1s = salloc("w1s", [128, EN, 128], BF16, side="right")
    w2s = salloc("w2s", [128, EN, 128], BF16, side="right")
    ghs = salloc("ghs", [128, EN, 512], BF16, side="right")
    p.op("gpsimd", lambda g: g.dma_start(out=w1s[:, :, :], in_=w1t_d),
         writes=["w1s"], sem="dw", inck=16)
    p.op("gpsimd", lambda g: g.dma_start(out=w2s[:, :, :], in_=w2t_d),
         writes=["w2s"], sem="dw", inck=16)
    for e in range(EN):
        p.op("vector", lambda v, e=e: v.tensor_scalar(
            w2s[:, e, :], w2s[:, e, :], gbc[:, e:e + 1], None, OP.mult),
             reads=["w2s", "gbc"], writes=["w2s"])

    for hh in range(DC):
        for tch in range(2):
            sl = slice(tch * 512, tch * 512 + 512)
            hb = tch % 2
            ob = 4 + (tch % 2)
            for e in range(EN):
                p.op("tensor", lambda t, hh=hh, e=e, sl=sl, hb=hb: t.matmul(
                    pbn[hb][:, :], w1s[:, e, :], xn_ffn[:, hh, sl],
                    start=True, stop=True),
                     reads=["w1s", "xn_ffn"], writes=[f"pb{hb}"])
                p.op("scalar", lambda sc, e=e, hb=hb: sc.activation(
                    ghs[:, e, :], pbn[hb][:, :], AF.Gelu,
                    bias=b1s[:, e:e + 1], scale=1.0),
                     reads=[f"pb{hb}", "b1s"], writes=["ghs"])
            for e in range(EN):
                p.op("tensor", lambda t, e=e, ob=ob: t.matmul(
                    pbn[ob][:, :], w2s[:, e, :], ghs[:, e, :],
                    start=(e == 0), stop=(e == EN - 1)),
                     reads=["w2s", "ghs"], writes=[f"pb{ob}"])
            p.op("vector", lambda v, hh=hh, sl=sl, ob=ob: v.scalar_tensor_tensor(
                u_sb[:, hh, sl], pbn[ob][:, :], b2g[:, :], xn_ffn[:, hh, sl],
                OP.add, OP.add),
                 reads=[f"pb{ob}", "b2g", "xn_ffn"], writes=["u_sb"])
    p.barrier()
    sfree("ghs", "w2s", "w1s")
    sfree("xn_ffn")

    g_sb = salloc("g_sb", [128, DC, TW], BF16, side="right")
    w3cb = [salloc(f"w3cb{i}", [128, DC, 128], BF16, side="right")
            for i in range(2)]
    for doc in range(DC):
        wi = doc % 2
        p.op("gpsimd", lambda g, doc=doc, wi=wi: g.dma_start(
            out=w3cb[wi][:, :, :], in_=w3t_d[doc]),
             writes=[f"w3cb{wi}"], sem="dw", inck=16)
        for tch in range(2):
            sl = slice(tch * 512, tch * 512 + 512)
            bk = tch % 2
            for dc in range(DC):
                p.op("tensor", lambda t, dc=dc, sl=sl, bk=bk, wi=wi: t.matmul(
                    pbn[bk][:, :], w3cb[wi][:, dc, :],
                    u_sb[:, dc, sl], start=(dc == 0), stop=(dc == DC - 1)),
                     reads=[f"w3cb{wi}", "u_sb"], writes=[f"pb{bk}"])
            p.op("scalar", lambda sc, doc=doc, sl=sl, bk=bk: sc.activation(
                g_sb[:, doc, sl], pbn[bk][:, :], AF.Silu,
                bias=b3s[:, doc:doc + 1], scale=1.0),
                 reads=[f"pb{bk}", "b3s"], writes=["g_sb"])
    p.barrier()
    sfree("w3cb1", "w3cb0")

    sg_sb = salloc("sg_sb", [128, DC, TW], BF16, side="right")
    for doc in range(DC):
        p.op("vector", lambda v, doc=doc: v.tensor_mul(
            sg_sb[:, doc, :], s_sb[:, doc, :], g_sb[:, doc, :]),
             reads=["s_sb", "g_sb"], writes=["sg_sb"])
    xrb = salloc("xrb", [128, D], F32, side="right")
    otb = [salloc(f"otb{i}", [128, D], F32, side="right") for i in range(2)]
    sgt = salloc("sgt", [128, D], BF16, side="right")
    x_ffn_t = x_ffn.rearrange("(n p) d -> n p d", p=128)
    for i in range(NTH):
        bi = i % 2
        p.op("sync", lambda s, i=i: s.dma_start(out=xrb[:, :], in_=x_ffn_t[i]),
             writes=["xrb"], sem="dio", inck=16)
        for half in range(2):
            for j in range(8):
                dc = half * 8 + j
                p.op("tensor", lambda t, i=i, dc=dc, j=j: t.transpose(
                    pTn[:, j * 128:(j + 1) * 128],
                    sg_sb[:, dc, i * 128:(i + 1) * 128], i128[:, :]),
                     reads=["sg_sb", "i128"], writes=["pT"])
            p.op("vector", lambda v, half=half: v.tensor_copy(
                sgt[:, half * 1024:half * 1024 + 1024], pTn[:, :]),
                 reads=["pT"], writes=["sgt"])
        p.op("vector", lambda v, bi=bi: v.tensor_add(otb[bi][:, :], xrb[:, :],
                                                     sgt[:, :]),
             reads=["xrb", "sgt"], writes=[f"otb{bi}"])
        p.op("sync", lambda s, i=i, bi=bi: s.dma_start(out=out_d[i],
                                                       in_=otb[bi][:, :]),
             reads=[f"otb{bi}"], writes=["out_d"], sem="dio", inck=16)
    if DBG:
        dbx = salloc("dbx", [128, TW], BF16, side="right")

        def dump(dst_ap, src_ap, names):
            n = src_ap.shape[-1]
            p.op("vector", lambda v: v.tensor_copy(dbx[:, 0:n], src_ap),
                 reads=list(names), writes=["dbx"])
            p.op("gpsimd", lambda g: g.dma_start(out=dst_ap, in_=dbx[:, 0:n]),
                 reads=["dbx"], writes=["dbgo"], sem="dw", inck=16)

        for oc in range(OC):
            dump(dbg_yb[oc], yb[:, oc, :], ["yb"])
        for st in range(NT):
            p.op("gpsimd", lambda g, st=st: g.dma_start(
                out=dbg_xn[st], in_=xt_dram[st]),
                 reads=["xt_dram"], writes=["dbgo"], sem="dw", inck=16)
        for oc in range(OC):
            for kt in range(NT):
                p.op("gpsimd", lambda g, oc=oc, kt=kt: g.dma_start(
                    out=dbg_kp[oc, kt], in_=kp_dram[oc, kt]),
                     reads=["kp_dram"], writes=["dbgo"], sem="dw", inck=16)
                p.op("gpsimd", lambda g, oc=oc, kt=kt: g.dma_start(
                    out=dbg_sv[oc, kt], in_=svt_dram[oc, kt]),
                     reads=["svt_dram"], writes=["dbgo"], sem="dw", inck=16)
        for doc in range(DC):
            dump(dbg_s[doc], s_sb[:, doc, :], ["s_sb"])
            dump(dbg_g[doc], g_sb[:, doc, :], ["g_sb"])
        p.op("vector", lambda v: v.tensor_copy(dbx[:, 0:OC], mcols[:, :]),
             reads=["mcols"], writes=["dbx"])
        p.op("vector", lambda v: v.tensor_copy(dbx[:, 32:64], msc[:, :]),
             reads=["msc"], writes=["dbx"])
        p.op("vector", lambda v: v.tensor_copy(dbx[0:1, 64:72], lgrow[0:1, :]),
             reads=["lgrow"], writes=["dbx"])
        p.op("vector", lambda v: v.tensor_copy(dbx[0:1, 72:80], gates[0:1, :]),
             reads=["gates"], writes=["dbx"])
        p.op("vector", lambda v: v.tensor_copy(dbx[0:1, 80:88], probs[0:1, :]),
             reads=["probs"], writes=["dbx"])
        p.op("gpsimd", lambda g: g.dma_start(out=dbg_sc, in_=dbx[:, 0:256]),
             reads=["dbx"], writes=["dbgo"], sem="dw", inck=16)
        p.op("gpsimd", lambda g: g.dma_start(out=dbg_tab[0], in_=cos_t[:, :]),
             reads=["cos_t"], writes=["dbgo"], sem="dw", inck=16)
        p.op("gpsimd", lambda g: g.dma_start(out=dbg_tab[1], in_=sin_t[:, :]),
             reads=["sin_t"], writes=["dbgo"], sem="dw", inck=16)
        p.op("gpsimd", lambda g: g.dma_start(out=dbg_tab[2][:, 0:TW],
                                             in_=cos_q[:, :]),
             reads=["cos_q"], writes=["dbgo"], sem="dw", inck=16)
        p.op("gpsimd", lambda g: g.dma_start(out=dbg_tab[3][:, 0:TW],
                                             in_=sin_q[:, :]),
             reads=["sin_q"], writes=["dbgo"], sem="dw", inck=16)
        p.op("gpsimd", lambda g: g.dma_start(out=dbg_pm[0], in_=pmat[:, :]),
             reads=["pmat"], writes=["dbgo"], sem="dw", inck=16)
        p.op("gpsimd", lambda g: g.dma_start(out=dbg_pm[1], in_=i128[:, :]),
             reads=["i128"], writes=["dbgo"], sem="dw", inck=16)
    fin = p.counts["dio"]

    p.burst_map()
    sems = {}
    sem_cms = []
    for n in ("ph", "pe", "act", "dve", "pool", "dio", "dw"):
        cm = nc.semaphore(n)
        sems[n] = cm.__enter__()
        sem_cms.append(cm)
    with nc.allow_low_precision("bf16 intermediates are deliberate"), \
            nc.Block() as block:
        @block.sync
        def _(sync):
            p.emit("sync", sync, sems)
            sync.wait_ge(sems["dio"], p.bm.get("dio", {}).get(fin, fin))

        @block.tensor
        def _(tensor):
            p.emit("tensor", tensor, sems)

        @block.vector
        def _(vector):
            p.emit("vector", vector, sems)

        @block.scalar
        def _(scalar):
            p.emit("scalar", scalar, sems)

        @block.gpsimd
        def _(gpsimd):
            p.emit("gpsimd", gpsimd, sems)

    for name in reversed(list(live)):
        live.pop(name).__exit__(None, None, None)
    for cm in sem_cms:
        cm.__exit__(None, None, None)
    return nc


_NC_CACHE = None
_LAST_RES = None


def _get_nc():
    global _NC_CACHE
    if _NC_CACHE is None:
        _NC_CACHE = build_program()
    return _NC_CACHE


def kernel(L0_attn_x, L0_attn_Wq, L0_attn_bq, L0_attn_Wk, L0_attn_bk,
           L0_attn_Wv, L0_attn_bv, L0_attn_Wo, L0_attn_bo,
           L0_ffn_switch_w, L0_ffn_switch_b, L0_ffn_w1, L0_ffn_b1,
           L0_ffn_w2, L0_ffn_b2, L0_ffn_w3, L0_ffn_b3):
    f = np.float32
    x = np.ascontiguousarray(np.asarray(L0_attn_x, dtype=f))
    Wq = np.asarray(L0_attn_Wq, dtype=f)
    Wk = np.asarray(L0_attn_Wk, dtype=f)
    Wv = np.asarray(L0_attn_Wv, dtype=f)
    Wo = np.asarray(L0_attn_Wo, dtype=f)
    sw = np.asarray(L0_ffn_switch_w, dtype=f)
    w1 = np.asarray(L0_ffn_w1, dtype=f)
    w2 = np.asarray(L0_ffn_w2, dtype=f)
    w3 = np.asarray(L0_ffn_w3, dtype=f)

    def blkw(WT):
        return np.ascontiguousarray(
            WT.reshape(DC, 128, OC, 128).transpose(2, 1, 0, 3))

    wqb = blkw(Wq.T)
    wkb = blkw(Wk.T)
    wvb = blkw(Wv.T)
    wob = np.ascontiguousarray(
        Wo.T.reshape(OC, 128, DC, 128).transpose(2, 1, 0, 3))
    w3t = np.ascontiguousarray(
        w3.T.reshape(DC, 128, DC, 128).transpose(2, 1, 0, 3))
    w1t = np.ascontiguousarray(w1.transpose(2, 0, 1))
    w2t = np.ascontiguousarray(w2.transpose(2, 0, 1))
    swb = np.ascontiguousarray(sw.reshape(EN, NT, 128, D))
    bq_c = np.ascontiguousarray(np.asarray(L0_attn_bq, f).reshape(OC, 128).T)
    bk_c = np.ascontiguousarray(np.asarray(L0_attn_bk, f).reshape(OC, 128).T)
    bv_c = np.ascontiguousarray(np.asarray(L0_attn_bv, f).reshape(OC, 128).T)
    bo_c = np.ascontiguousarray(np.asarray(L0_attn_bo, f).reshape(DC, 128).T)
    b3_c = np.ascontiguousarray(np.asarray(L0_ffn_b3, f).reshape(DC, 128).T)
    b1_c = np.ascontiguousarray(np.asarray(L0_ffn_b1, f).T)
    b2t = np.ascontiguousarray(np.asarray(L0_ffn_b2, f))
    swb_r = np.ascontiguousarray(np.asarray(L0_ffn_switch_b, f).reshape(1, EN))
    pos = np.arange(S, dtype=f).reshape(1, S)

    in_maps = []
    for c in range(8):
        b, r = c // 2, c % 2
        own = [r + 2 * i for i in range(NTH)]
        rows = np.concatenate([np.arange(g * 128, (g + 1) * 128) for g in own])
        posq = rows.astype(f).reshape(1, TW)
        in_maps.append({
            "x_attn": x[b],
            "x_ffn": np.ascontiguousarray(x[b][rows, :]),
            "wq_b": wqb, "wk_b": wkb, "wv_b": wvb,
            "wo_b": wob, "w3t": w3t, "w1t": w1t, "w2t": w2t,
            "sw_b": swb,
            "bq_c": bq_c, "bk_c": bk_c, "bv_c": bv_c,
            "bo_c": bo_c, "b3_c": b3_c, "b1_c": b1_c, "b2t": b2t,
            "swb": swb_r, "pos": pos, "posq": posq,
            "parc": np.full((128, 1), float(r), f),
        })

    nc = _get_nc()
    res = run_bass_kernel_spmd(nc, in_maps, core_ids=list(range(8)))
    global _LAST_RES
    _LAST_RES = res
    out = np.empty((4, S, D), dtype=np.float32)
    for c in range(8):
        b, r = c // 2, c % 2
        for i in range(NTH):
            g = r + 2 * i
            out[b, g * 128:(g + 1) * 128, :] = res.results[c]["out"][i]
    return out


if __name__ == "__main__":
    build_program()
    print("build ok")


# version 25
# speedup vs baseline: 1.1679x; 1.0223x over previous
import os

import numpy as np

import concourse.bass as bass
import concourse.mybir as mybir
from concourse.bass_utils import run_bass_kernel_spmd

F32 = mybir.dt.float32
BF16 = mybir.dt.bfloat16
AF = mybir.ActivationFunctionType
OP = mybir.AluOpType
AX = mybir.AxisListType

DBG = bool(int(os.environ.get("MK_DEBUG", "0")))

D = 2048
S = 2048
NT = 16
NTH = 8
TW = 1024
OC = 16
DC = 16
EN = 8
PI = float(np.pi)
LOG1E4 = float(np.log(10000.0))
BIG = 1.0e30

ENGS = ("sync", "tensor", "vector", "scalar", "gpsimd")
ESEM = {"sync": "dio", "tensor": "pe", "vector": "dve", "scalar": "act",
        "gpsimd": "dw"}
DMASEMS = ("dio", "dw")


class Ev:
    __slots__ = ("sem", "val")

    def __init__(self, sem, val):
        self.sem = sem
        self.val = val


class Prog:

    def __init__(self):
        self.q = {e: [] for e in ENGS}
        self.counts = {"ph": 0}
        self.phase_n = 0
        self.bufw = {}
        self.bufr = {}

    def op(self, eng, fn, reads=(), writes=(), sem=None, inck=1):
        sem = sem or ESEM[eng]
        is_dma = sem in DMASEMS
        waits = []
        for b in reads:
            w = self.bufw.get(b)
            if w is not None:
                waits.append(w)
        for b in writes:
            w = self.bufw.get(b)
            if w is not None:
                waits.append(w)
            waits.extend(self.bufr.get(b, ()))
        pruned = {}
        for w in waits:
            wdma = w.sem in DMASEMS
            if w.sem == sem == "pe":
                continue
            if pruned.get(w.sem, -1) < w.val:
                pruned[w.sem] = w.val
        self.counts[sem] = self.counts.get(sem, 0) + inck
        ev = Ev(sem, self.counts[sem])
        self.q[eng].append(("op", fn, tuple(Ev(s, v) for s, v in pruned.items()),
                            sem, inck))
        for b in reads:
            self.bufr.setdefault(b, []).append(ev)
        for b in writes:
            self.bufw[b] = ev
            self.bufr[b] = []
        return ev

    def barrier(self):
        self.phase_n += 1
        self.counts["ph"] += len(ENGS)
        tgt = self.phase_n * len(ENGS)
        for e in ENGS:
            self.q[e].append(("bar", tgt))
        ev = Ev("ph", tgt)
        for b in list(self.bufw):
            self.bufw[b] = ev
        self.bufr = {}

    def emit(self, eng_name, engobj, sems):
        waited = {}
        for item in self.q[eng_name]:
            if item[0] == "bar":
                engobj.sem_inc(sems["ph"], 1)
                engobj.wait_ge(sems["ph"], item[1])
                waited = {"ph": item[1]}
                continue
            _, fn, waits, sem, inck = item
            for w in waits:
                if waited.get(w.sem, -1) >= w.val:
                    continue
                engobj.wait_ge(sems[w.sem], w.val)
                waited[w.sem] = w.val
            res = fn(engobj)
            last = res[-1] if isinstance(res, (list, tuple)) else res
            last.then_inc(sems[sem], inck)


def build_program():
    nc = bass.Bass("TRN2", target_bir_lowering=False)

    def din(name, shape):
        return nc.dram_tensor(name, shape, F32, kind="ExternalInput").ap()

    x_attn = din("x_attn", [S, D])
    x_ffn = din("x_ffn", [TW, D])
    wq_b = din("wq_b", [OC, 128, DC, 128])
    wk_b = din("wk_b", [OC, 128, DC, 128])
    wv_b = din("wv_b", [OC, 128, DC, 128])
    wo_b = din("wo_b", [DC, 128, OC, 128])
    w3t_d = din("w3t", [DC, 128, DC, 128])
    w1t_d = din("w1t", [128, EN, 128])
    w2t_d = din("w2t", [128, EN, 128])
    sw_d = din("sw_b", [EN, NT, 128, D])
    bq_d = din("bq_c", [128, OC])
    bk_d = din("bk_c", [128, OC])
    bv_d = din("bv_c", [128, OC])
    bo_d = din("bo_c", [128, DC])
    b3_d = din("b3_c", [128, DC])
    b1_d = din("b1_c", [128, EN])
    b2t_d = din("b2t", [EN, 128])
    swb_d = din("swb", [1, EN])
    pos_d = din("pos", [1, S])
    posq_d = din("posq", [1, TW])
    parc_d = din("parc", [128, 1])
    out_d = nc.dram_tensor("out", [NTH, 128, D], F32, kind="ExternalOutput").ap()
    if DBG:
        dbg_qp = nc.dram_tensor("dbg_qp", [OC, 128, TW], F32,
                                kind="ExternalOutput").ap()
        dbg_kp = nc.dram_tensor("dbg_kp", [OC, NT, 128, 128], F32,
                                kind="ExternalOutput").ap()
        dbg_sv = nc.dram_tensor("dbg_sv", [OC, NT, 128, 128], F32,
                                kind="ExternalOutput").ap()
        dbg_xn = nc.dram_tensor("dbg_xn", [NT, 128, D], F32,
                                kind="ExternalOutput").ap()
        dbg_yb = nc.dram_tensor("dbg_yb", [OC, 128, TW], F32,
                                kind="ExternalOutput").ap()
        dbg_s = nc.dram_tensor("dbg_s", [DC, 128, TW], F32,
                               kind="ExternalOutput").ap()
        dbg_g = nc.dram_tensor("dbg_g", [DC, 128, TW], F32,
                               kind="ExternalOutput").ap()
        dbg_u = nc.dram_tensor("dbg_u", [DC, 128, TW], F32,
                               kind="ExternalOutput").ap()
        dbg_sc = nc.dram_tensor("dbg_sc", [128, 256], F32,
                                kind="ExternalOutput").ap()
        dbg_tab = nc.dram_tensor("dbg_tab", [4, 128, S], F32,
                                 kind="ExternalOutput").ap()
        dbg_pm = nc.dram_tensor("dbg_pm", [2, 128, 128], F32,
                                kind="ExternalOutput").ap()

    kp_dram = nc.dram_tensor("kp_dram", [OC, NT, 128, 128], BF16).ap()
    svt_dram = nc.dram_tensor("svt_dram", [OC, NT, 128, 130], BF16).ap()
    xt_dram = nc.dram_tensor("xt_dram", [NT, 128, D], BF16).ap()
    i32_dram = nc.dram_tensor("i32_dram", [32, 32], BF16).ap()

    p = Prog()
    live = {}

    def salloc(name, shape, dtype, side="left"):
        cm = nc.sbuf_tensor(name, shape, dtype, side=side)
        h = cm.__enter__()
        live[name] = cm
        return h

    def sfree(*names):
        for name in names:
            live.pop(name).__exit__(None, None, None)

    cos_t = salloc("cos_t", [128, S], BF16)
    sin_t = salloc("sin_t", [128, S], BF16)
    cos_q = salloc("cos_q", [128, TW], BF16)
    sin_q = salloc("sin_q", [128, TW], BF16)
    i128 = salloc("i128", [128, 128], BF16)
    pmat = salloc("pmat", [128, 128], BF16)
    tril = salloc("tril", [128, 128], BF16)
    am = salloc("am", [128, 2, 128], BF16)
    ones_col = salloc("ones_col", [128, 1], BF16)
    ones_row = salloc("ones_row", [1, 128], BF16)
    ones_row_f = salloc("ones_row_f", [1, 128], F32)
    negpi = salloc("negpi", [128, 1], F32)
    epsc = salloc("epsc", [128, 1], F32)
    parc = salloc("parcs", [128, 1], F32)
    bqs = salloc("bqs", [128, OC], F32)
    bks = salloc("bks", [128, OC], F32)
    bvh = salloc("bvh", [128, OC], F32)
    bos = salloc("bos", [128, DC], F32)
    b3s = salloc("b3s", [128, DC], F32)
    b1s = salloc("b1s", [128, EN], F32)
    b2ts = salloc("b2ts", [EN, 128], F32)
    swbs = salloc("swbs", [1, EN], F32)

    mcols = salloc("mcols", [128, OC], F32)
    lnsc = salloc("lnsc", [128, 12], F32)
    msc = salloc("msc", [128, 32], F32)
    mscb = salloc("mscb", [128, 32], BF16)
    mrow = salloc("mrow", [1, 64], F32)
    lgrow = salloc("lgrow", [1, EN], F32)
    probs = salloc("probs", [1, EN], F32)
    pscr = salloc("pscr", [1, EN], F32)
    psc2 = salloc("psc2", [1, 4], F32)
    gates = salloc("gates", [1, EN], F32)
    gbc = salloc("gbc", [128, EN], F32)
    gcol = salloc("gcol", [EN, 1], F32)
    b2g = salloc("b2g", [128, 1], F32)
    rr = salloc("rr", [1, 512], BF16)

    xn_ffn = salloc("xn_ffn", [128, DC, TW], BF16)
    qp = salloc("qp", [128, OC, TW], BF16)

    def palloc(name, shape, dtype):
        cm = nc.psum_tensor(name, shape, dtype)
        h = cm.__enter__()
        live[name] = cm
        return h

    pb = [palloc(f"pb{i}", [128, 512], F32) for i in range(6)]
    pT = palloc("pT", [128, 1024], BF16)
    pS = palloc("pS", [128, 512], F32)

    p.op("gpsimd", lambda g: g.iota(i128[:, :], [[1, 128]], channel_multiplier=-1,
                                    base=0, allow_small_or_imprecise_dtypes=True),
         writes=["i128"], sem="pool")
    p.op("vector", lambda v: v.tensor_single_scalar(i128[:, :], i128[:, :], 0.0,
                                                    OP.is_equal),
         reads=["i128"], writes=["i128"])
    p.op("gpsimd", lambda g: g.iota(tril[:, :], [[1, 128]], channel_multiplier=-1,
                                    base=0, allow_small_or_imprecise_dtypes=True),
         writes=["tril"], sem="pool")
    p.op("vector", lambda v: v.tensor_single_scalar(tril[:, :], tril[:, :], 0.0,
                                                    OP.is_ge),
         reads=["tril"], writes=["tril"])
    p.op("vector", lambda v: v.memset(ones_col[:, :], 1.0), writes=["ones_col"])
    p.op("vector", lambda v: v.memset(ones_row[:, :], 1.0), writes=["ones_row"])
    p.op("vector", lambda v: v.memset(ones_row_f[:, :], 1.0),
         writes=["ones_row_f"])
    p.op("vector", lambda v: v.memset(negpi[:, :], -PI), writes=["negpi"])
    p.op("vector", lambda v: v.memset(epsc[:, :], 1e-5), writes=["epsc"])

    def ld(dstname, dst, src):
        return p.op("sync", lambda s: s.dma_start(out=dst, in_=src),
                    writes=[dstname], sem="dio", inck=16)

    ld("bqs", bqs[:, :], bq_d)
    ld("bks", bks[:, :], bk_d)
    ld("bvh", bvh[:, :], bv_d)
    ld("bos", bos[:, :], bo_d)
    ld("b3s", b3s[:, :], b3_d)
    ld("b1s", b1s[:, :], b1_d)
    ld("b2ts", b2ts[:, :], b2t_d)
    ld("swbs", swbs[:, :], swb_d)
    ld("parcs", parc[:, :], parc_d)
    p.op("vector", lambda v: v.tensor_single_scalar(bvh[:, :], bvh[:, :], 0.5,
                                                    OP.mult),
         reads=["bvh"], writes=["bvh"])
    p.op("vector", lambda v: v.tensor_scalar(am[:, 0, :], tril[:, :],
                                             parc[:, 0:1], None, OP.max),
         reads=["tril", "parcs"], writes=["am"])
    p.op("vector", lambda v: v.tensor_scalar(am[:, 1, :], tril[:, :],
                                             parc[:, 0:1], None, OP.min),
         reads=["tril", "parcs"], writes=["am"])

    p.op("vector", lambda v: v.memset(pmat[:, :], 0.0), writes=["pmat"])
    p.op("sync", lambda s: s.dma_start(out=i32_dram, in_=i128[0:32, 0:32]),
         reads=["i128"], writes=["i32d"], sem="dio", inck=16)
    for (pr, pc) in ((0, 32), (32, 0), (64, 96), (96, 64)):
        p.op("sync",
             lambda s, pr=pr, pc=pc: s.dma_start(
                 out=pmat[pr:pr + 32, pc:pc + 32], in_=i32_dram),
             reads=["i32d"], writes=["pmat"], sem="dio", inck=16)

    poss = salloc("poss", [1, S], F32, side="right")
    possq = salloc("possq", [1, TW], F32, side="right")
    ld("poss", poss[:, :], pos_d)
    ld("possq", possq[:, :], posq_d)
    invf = salloc("invf", [1, 128], F32, side="right")
    sgn = salloc("sgn", [128, 1], F32, side="right")
    sgr = salloc("sgr", [1, 128], F32, side="right")
    ang = salloc("ang", [128, S], F32, side="right")
    aki = salloc("aki", [128, 512], mybir.dt.int32, side="right")
    akf = salloc("akf", [128, 512], F32, side="right")
    akm = salloc("akm", [128, 512], F32, side="right")
    p.op("gpsimd", lambda g: g.iota(invf[:, :], [[0, 4], [1, 32]],
                                    channel_multiplier=0, base=0,
                                    allow_small_or_imprecise_dtypes=True),
         writes=["invf"], sem="pool")
    p.op("scalar", lambda sc: sc.activation(invf[:, :], invf[:, :], AF.Exp,
                                            bias=0.0, scale=-LOG1E4 / 32.0),
         reads=["invf"], writes=["invf"])
    p.op("gpsimd", lambda g: g.iota(sgr[:, :], [[0, 2], [1, 64]],
                                    channel_multiplier=0, base=0,
                                    allow_small_or_imprecise_dtypes=True),
         writes=["sgr"], sem="pool")
    p.op("vector", lambda v: v.tensor_single_scalar(sgr[:, :], sgr[:, :], 32.0,
                                                    OP.is_lt),
         reads=["sgr"], writes=["sgr"])
    p.op("vector", lambda v: v.tensor_scalar(sgr[:, :], sgr[:, :], -2.0, 1.0,
                                             OP.mult, OP.add),
         reads=["sgr"], writes=["sgr"])
    p.op("tensor", lambda t: t.matmul(pS[:, 0:1], sgr[0:1, :],
                                      ones_row_f[0:1, 0:1], start=True,
                                      stop=True),
         reads=["sgr", "ones_row_f"], writes=["pS"])
    p.op("vector", lambda v: v.tensor_copy(sgn[:, :], pS[:, 0:1]),
         reads=["pS"], writes=["sgn"])

    def angles(shift, posrow, posname, nw, dstname, dst):
        for ch in range(nw):
            sl = slice(ch * 512, ch * 512 + 512)
            p.op("tensor",
                 lambda t, sl=sl, ch=ch: t.matmul(pb[ch % 4][:, :], invf[0:1, :],
                                                  posrow[0:1, sl], start=True,
                                                  stop=True),
                 reads=["invf", posname], writes=[f"pb{ch % 4}"])
            p.op("vector", lambda v, ch=ch: v.tensor_scalar(
                akf[:, :], pb[ch % 4][:, :], shift, 1.0 / (2 * PI), OP.add,
                OP.mult),
                 reads=[f"pb{ch % 4}"], writes=["akf"])
            p.op("vector", lambda v: v.tensor_copy(aki[:, :], akf[:, :]),
                 reads=["akf"], writes=["aki"])
            p.op("vector", lambda v: v.tensor_copy(akf[:, :], aki[:, :]),
                 reads=["aki"], writes=["akf"])
            p.op("vector", lambda v, ch=ch: v.scalar_tensor_tensor(
                akf[:, :], akf[:, :], -2 * PI, pb[ch % 4][:, :], OP.mult,
                OP.add),
                 reads=["akf", f"pb{ch % 4}"], writes=["akf"])
            p.op("vector", lambda v: v.tensor_single_scalar(
                akf[:, :], akf[:, :], shift, OP.add),
                 reads=["akf"], writes=["akf"])
            p.op("vector", lambda v: v.tensor_single_scalar(
                akm[:, :], akf[:, :], PI, OP.is_gt),
                 reads=["akf"], writes=["akm"])
            p.op("vector", lambda v: v.scalar_tensor_tensor(
                akf[:, :], akm[:, :], -2 * PI, akf[:, :], OP.mult, OP.add),
                 reads=["akm", "akf"], writes=["akf"])
            p.op("vector", lambda v: v.tensor_single_scalar(
                akm[:, :], akf[:, :], -PI, OP.is_lt),
                 reads=["akf"], writes=["akm"])
            p.op("vector", lambda v, sl=sl: v.scalar_tensor_tensor(
                ang[:, sl], akm[:, :], 2 * PI, akf[:, :], OP.mult, OP.add),
                 reads=["akm", "akf"], writes=["ang"])
        p.op("scalar", lambda sc, nw=nw: sc.activation(
            dst[:, 0:nw * 512], ang[:, 0:nw * 512], AF.Sin, bias=0.0,
            scale=1.0),
             reads=["ang"], writes=[dstname])

    angles(0.0, poss, "poss", 4, "sin_t", sin_t)
    angles(PI / 2, poss, "poss", 4, "cos_t", cos_t)
    angles(0.0, possq, "possq", 2, "sin_q", sin_q)
    angles(PI / 2, possq, "possq", 2, "cos_q", cos_q)
    p.op("vector", lambda v: v.tensor_scalar(sin_t[:, :], sin_t[:, :],
                                             sgn[:, 0:1], None, OP.mult),
         reads=["sin_t", "sgn"], writes=["sin_t"])
    p.op("vector", lambda v: v.tensor_scalar(sin_q[:, :], sin_q[:, :],
                                             sgn[:, 0:1], None, OP.mult),
         reads=["sin_q", "sgn"], writes=["sin_q"])
    sfree("akm", "akf", "aki", "ang", "sgr", "sgn", "invf",
          "possq", "poss")
    p.barrier()

    xn_fm = salloc("xn_fm", [128, DC, S], BF16, side="right")
    xt = [salloc(f"xt{i}", [128, D], F32, side="right") for i in range(2)]
    xnt = salloc("xnt", [128, D], BF16, side="right")
    xsq = salloc("xsq", [128, D], BF16, side="right")

    def emit_ln(dstname, dst_fm, src_tiled, ntiles, to_dram):
        for i in range(ntiles):
            bi = i % 2
            bn = f"xt{bi}"
            c0 = 6 * bi
            p.op("sync", lambda s, i=i, bi=bi: s.dma_start(out=xt[bi][:, :],
                                                           in_=src_tiled[i]),
                 writes=[bn], sem="dio", inck=16)
            p.op("vector", lambda v, bi=bi, c0=c0: v.reduce_sum(
                lnsc[:, c0:c0 + 1], xt[bi][:, :], AX.X),
                 reads=[bn], writes=[f"ls{bi}"])
            p.op("scalar", lambda sc, bi=bi, c0=c0: sc.activation(
                xsq[:, :], xt[bi][:, :], AF.Square,
                accum_out=lnsc[:, c0 + 1:c0 + 2]),
                 reads=[bn], writes=["xsq", f"lq{bi}"])
            p.op("vector", lambda v, c0=c0: v.tensor_single_scalar(
                lnsc[:, c0:c0 + 1], lnsc[:, c0:c0 + 1], 1.0 / D, OP.mult),
                 reads=[f"ls{bi}"], writes=[f"ls{bi}"])
            p.op("vector", lambda v, c0=c0: v.tensor_mul(
                lnsc[:, c0 + 2:c0 + 3], lnsc[:, c0:c0 + 1],
                lnsc[:, c0:c0 + 1]),
                 reads=[f"ls{bi}"], writes=[f"lm{bi}"])
            p.op("vector", lambda v, c0=c0: v.scalar_tensor_tensor(
                lnsc[:, c0 + 3:c0 + 4], lnsc[:, c0 + 1:c0 + 2], 1.0 / D,
                lnsc[:, c0 + 2:c0 + 3], OP.mult, OP.subtract),
                 reads=[f"lq{bi}", f"lm{bi}"], writes=[f"lv{bi}"])
            p.op("scalar", lambda sc, c0=c0: sc.activation(
                lnsc[:, c0 + 4:c0 + 5], lnsc[:, c0 + 3:c0 + 4], AF.Sqrt,
                bias=epsc[:, 0:1], scale=1.0),
                 reads=[f"lv{bi}", "epsc"], writes=[f"lsd{bi}"])
            p.op("vector", lambda v, c0=c0: v.reciprocal(
                lnsc[:, c0 + 5:c0 + 6], lnsc[:, c0 + 4:c0 + 5]),
                 reads=[f"lsd{bi}"], writes=[f"lr{bi}"])
            p.op("vector", lambda v, bi=bi, c0=c0: v.tensor_scalar(
                xnt[:, :], xt[bi][:, :], lnsc[:, c0:c0 + 1],
                lnsc[:, c0 + 5:c0 + 6], OP.subtract, OP.mult),
                 reads=[bn, f"ls{bi}", f"lr{bi}"], writes=["xnt"])
            if to_dram:
                p.op("sync", lambda s, i=i: s.dma_start(out=xt_dram[i],
                                                        in_=xnt[:, :]),
                     reads=["xnt"], writes=["xt_dram"], sem="dio", inck=16)
            for half in range(2):
                for j in range(8):
                    dc = half * 8 + j
                    p.op("tensor", lambda t, dc=dc, j=j: t.transpose(
                        pT[:, j * 128:(j + 1) * 128],
                        xnt[:, dc * 128:(dc + 1) * 128], i128[:, :]),
                         reads=["xnt", "i128"], writes=["pT"])
                p.op("vector", lambda v, i=i, half=half: v.tensor_copy(
                    dst_fm[:, half * 8:half * 8 + 8, i * 128:(i + 1) * 128],
                    pT[:, :].rearrange("p (dc t) -> p dc t", t=128)),
                     reads=["pT"], writes=[dstname])

    emit_ln("xn_fm", xn_fm, x_attn.rearrange("(n p) d -> n p d", p=128), NT,
            True)
    emit_ln("xn_ffn", xn_ffn, x_ffn.rearrange("(n p) d -> n p d", p=128), NTH,
            False)
    sfree("xsq", "xnt", "xt1", "xt0")
    p.barrier()

    wqs1 = salloc("wqs0", [128, DC, 128], BF16, side="right")
    wks1 = salloc("wks0", [128, DC, 128], BF16, side="right")
    wvs = salloc("wvs", [128, DC, 128], BF16, side="right")
    wqs = [wqs1, wqs1]
    wks = [wks1, wks1]
    ra = salloc("ra", [128, S], BF16, side="right")
    rb = salloc("rb", [128, S], BF16, side="right")
    kpb = salloc("kpb", [128, S], BF16, side="right")
    svq = salloc("svq", [128, 8, 130], BF16, side="right")

    def wload(dst, dstname, src):
        p.op("gpsimd", lambda g: g.dma_start(out=dst[:, :, :], in_=src),
             writes=[dstname], sem="dw", inck=16)

    p.op("vector", lambda v: v.memset(svq[:, :, 64:65], 1.0), writes=["svqo"])
    p.op("vector", lambda v: v.memset(svq[:, :, 129:130], 1.0), writes=["svqo"])

    def rope(src, srcname, cosname, cost, sinname, sint, dst_fn, dstname, nw):
        for tch in range(nw):
            sl = slice(tch * 512, tch * 512 + 512)
            rbk = 4 + tch % 2
            p.op("tensor", lambda t, sl=sl, rbk=rbk: t.matmul(
                pb[rbk][:, :], pmat[:, :], src[:, sl], start=True, stop=True),
                 reads=[srcname, "pmat"], writes=[f"pb{rbk}"])
            p.op("vector", lambda v, sl=sl, rbk=rbk: v.tensor_mul(
                rb[:, sl], pb[rbk][:, :], sint[:, sl]),
                 reads=[f"pb{rbk}", sinname], writes=["rb"])
            p.op("vector", lambda v, sl=sl: v.tensor_mul(
                src[:, sl], src[:, sl], cost[:, sl]),
                 reads=[srcname, cosname], writes=[srcname])
            p.op("vector", lambda v, sl=sl: v.scalar_tensor_tensor(
                dst_fn(sl), src[:, sl], 1.0, rb[:, sl], OP.mult, OP.add),
                 reads=[srcname, "rb"], writes=[dstname])

    rv = salloc("rv", [128, S], BF16, side="right")
    wload(wvs, "wvs", wv_b[0])
    wload(wqs[0], "wqs0", wq_b[0])
    wload(wks[0], "wks0", wk_b[0])
    for oc in range(OC):
        for dc in range(DC):
            for tch in range(4):
                sl = slice(tch * 512, tch * 512 + 512)
                p.op("tensor", lambda t, dc=dc, tch=tch, sl=sl:
                     t.matmul(pb[tch][:, :], wvs[:, dc, :], xn_fm[:, dc, sl],
                              start=(dc == 0), stop=(dc == DC - 1)),
                     reads=["wvs", "xn_fm"], writes=[f"pb{tch}"])
        if oc + 1 < OC:
            wload(wvs, "wvs", wv_b[oc + 1])
        for dc in range(DC):
            for tch in range(2):
                sl = slice(tch * 512, tch * 512 + 512)
                p.op("tensor", lambda t, dc=dc, tch=tch, sl=sl:
                     t.matmul(pb[4 + tch][:, :], wqs[0][:, dc, :],
                              xn_ffn[:, dc, sl],
                              start=(dc == 0), stop=(dc == DC - 1)),
                     reads=["wqs0", "xn_ffn"], writes=[f"pb{4 + tch}"])
        if oc + 1 < OC:
            wload(wqs[0], "wqs0", wq_b[oc + 1])
        for tch in range(4):
            sl = slice(tch * 512, tch * 512 + 512)
            p.op("scalar", lambda sc, tch=tch, sl=sl, oc=oc: sc.activation(
                rv[:, sl], pb[tch][:, :], AF.Tanh,
                bias=bvh[:, oc:oc + 1], scale=0.5),
                 reads=[f"pb{tch}", "bvh"], writes=["rv"])
            p.op("vector", lambda v, sl=sl: v.tensor_scalar(
                rv[:, sl], rv[:, sl], 0.5, 0.5, OP.mult, OP.add),
                 reads=["rv"], writes=["rv"])
        for dc in range(DC):
            for tch in range(4):
                sl = slice(tch * 512, tch * 512 + 512)
                p.op("tensor", lambda t, dc=dc, tch=tch, sl=sl:
                     t.matmul(pb[tch][:, :], wks[0][:, dc, :],
                              xn_fm[:, dc, sl],
                              start=(dc == 0), stop=(dc == DC - 1)),
                     reads=["wks0", "xn_fm"], writes=[f"pb{tch}"])
        if oc + 1 < OC:
            wload(wks[0], "wks0", wk_b[oc + 1])
        for tch in range(2):
            sl = slice(tch * 512, tch * 512 + 512)
            p.op("scalar", lambda sc, tch=tch, sl=sl, oc=oc: sc.activation(
                ra[:, sl], pb[4 + tch][:, :], AF.Identity,
                bias=bqs[:, oc:oc + 1], scale=1.0),
                 reads=[f"pb{4 + tch}", "bqs"], writes=["ra"])
        rope(ra, "ra", "cos_q", cos_q, "sin_q", sin_q,
             lambda sl, oc=oc: qp[:, oc, sl], "qp", 2)
        p.op("vector", lambda v, oc=oc: v.scalar_tensor_tensor(
            rb[:, 0:TW], qp[:, oc, :], 1.0, qp[:, oc, :], OP.mult, OP.mult,
            accum_out=msc[:, oc:oc + 1]),
             reads=["qp"], writes=["rb", "msc"])
        for half in range(2):
            for j in range(8):
                kt = half * 8 + j
                p.op("tensor", lambda t, kt=kt, j=j: t.transpose(
                    pT[:, j * 128:(j + 1) * 128], rv[:, kt * 128:(kt + 1) * 128],
                    i128[:, :]),
                     reads=["rv", "i128"], writes=["pT"])
            p.op("vector", lambda v: v.tensor_copy(
                svq[:, :, :].rearrange("p kt (blk o) -> p kt blk o", o=65)
                [:, :, :, 0:64],
                pT[:, :].rearrange("p (kt blk o) -> p kt blk o", o=64, blk=2)),
                 reads=["pT"], writes=["svq"])
            for j in range(8):
                kt = half * 8 + j
                p.op("sync", lambda s, oc=oc, kt=kt, j=j: s.dma_start(
                    out=svt_dram[oc, kt], in_=svq[:, j, :]),
                     reads=["svq", "svqo"], writes=["svt_dram"], sem="dio",
                     inck=16)
        for tch in range(4):
            sl = slice(tch * 512, tch * 512 + 512)
            p.op("scalar", lambda sc, tch=tch, sl=sl, oc=oc: sc.activation(
                ra[:, sl], pb[tch][:, :], AF.Identity,
                bias=bks[:, oc:oc + 1], scale=1.0),
                 reads=[f"pb{tch}", "bks"], writes=["ra"])
        rope(ra, "ra", "cos_t", cos_t, "sin_t", sin_t,
             lambda sl: kpb[:, sl], "kpb", 4)
        p.op("vector", lambda v, oc=oc: v.scalar_tensor_tensor(
            rb[:, :], kpb[:, :], 1.0, kpb[:, :], OP.mult, OP.mult,
            accum_out=msc[:, OC + oc:OC + oc + 1]),
             reads=["kpb"], writes=["rb", "msc"])
        for kt in range(NT):
            p.op("sync", lambda s, oc=oc, kt=kt: s.dma_start(
                out=kp_dram[oc, kt], in_=kpb[:, kt * 128:(kt + 1) * 128]),
                 reads=["kpb"], writes=["kp_dram"], sem="dio", inck=16)

    p.op("vector", lambda v: v.tensor_copy(mscb[:, :], msc[:, :]),
         reads=["msc"], writes=["mscb"])
    p.op("tensor", lambda t: t.matmul(pS[0:1, 0:32], ones_col[:, :], mscb[:, :],
                                      start=True, stop=True),
         reads=["mscb", "ones_col"], writes=["pS"])
    p.op("vector", lambda v: v.tensor_copy(mrow[0:1, 0:32], pS[0:1, 0:32]),
         reads=["pS"], writes=["mrow"])
    p.op("vector", lambda v: v.scalar_tensor_tensor(
        mrow[0:1, 32:48], mrow[0:1, 0:16], 2.0, mrow[0:1, 16:32],
        OP.mult, OP.add),
         reads=["mrow"], writes=["mrow2"])
    p.op("tensor", lambda t: t.matmul(pS[:, 0:16], ones_row_f[0:1, :],
                                      mrow[0:1, 32:48], start=True, stop=True),
         reads=["mrow2", "ones_row_f"], writes=["pS"])
    p.op("vector", lambda v: v.tensor_scalar(mcols[:, :], pS[:, 0:16],
                                             -1.0 / 32768.0, None, OP.mult),
         reads=["pS"], writes=["mcols"])

    sfree("rv", "svq", "kpb", "rb", "ra", "wvs", "wks0", "wqs0", "xn_fm")
    p.barrier()

    sfree("pS", "pT", "pb5", "pb4", "pb3", "pb2", "pb1", "pb0")
    sc = [palloc(f"sc{i}", [128, 1024], F32) for i in range(2)]
    yB = [palloc(f"yB{i}", [128, 512], F32) for i in range(4)]

    yb = salloc("yb", [128, OC, TW], BF16, side="right")
    kps = [salloc(f"kps{i}", [128, NT, 128], BF16, side="right")
           for i in range(2)]
    svs = [salloc(f"svs{i}", [128, NT, 130], BF16, side="right")
           for i in range(2)]
    expb = [salloc(f"expb{i}", [128, 1024], BF16, side="right")
            for i in range(3)]
    ybig = [salloc(f"ybig{i}", [128, 4, 512], BF16, side="right")
            for i in range(2)]
    swl = [salloc(f"swl{i}", [128, D], BF16, side="right") for i in range(2)]
    xnl = [salloc(f"xnl{i}", [128, D], BF16, side="right") for i in range(2)]
    swp = salloc("swp", [128, D], BF16, side="right")
    racc = salloc("racc", [128, 512], F32, side="right")
    raccb = salloc("raccb", [128, 512], BF16, side="right")
    mrow2 = salloc("mrow2", [1, 512], F32, side="right")

    rt_iter = [0]

    def router_chunk():
        it = rt_iter[0]
        if it >= EN * NT * 4:
            return
        rt_iter[0] += 1
        pc = it % 4
        st, e = (it // 4) // EN, (it // 4) % EN
        rbi = (it // 4) % 2
        xbi = (st % 2)
        if e == 0 and pc == 0:
            p.op("sync", lambda s, st=st, xbi=xbi: s.dma_start(
                out=xnl[xbi][:, :], in_=xt_dram[st]),
                 reads=["xt_dram"], writes=[f"xnl{xbi}"], sem="dio", inck=16)
        if pc == 0:
            p.op("gpsimd", lambda g, e=e, st=st, rbi=rbi: g.dma_start(
                out=swl[rbi][:, :], in_=sw_d[e, st]),
                 writes=[f"swl{rbi}"], sem="dw", inck=16)
        sl = slice(pc * 512, pc * 512 + 512)
        p.op("vector", lambda v, rbi=rbi, xbi=xbi, it=it, sl=sl:
             v.scalar_tensor_tensor(
                 swp[:, sl], xnl[xbi][:, sl], 1.0, swl[rbi][:, sl],
                 OP.mult, OP.mult, accum_out=racc[:, it:it + 1]),
             reads=[f"swl{rbi}", f"xnl{xbi}"], writes=["swp", "racc"])

    scnt = [0]
    rrf = salloc("rrf", [1, 512], F32, side="right")
    pend_norm = []

    def emit_scores(hp, bi, kt, h2):
        wstart = (kt // 2) * 128
        hbase = h2 * 64
        sb = sc[scnt[0] % 2]
        sn = f"sc{scnt[0] % 2}"
        scnt[0] += 1
        if wstart < 512:
            mms = [(wstart, 512 - wstart), (512, 512)]
        else:
            mms = [(wstart, TW - wstart)]
        for (qoff, n) in mms:
            p.op("tensor", lambda t, hp=hp, bi=bi, kt=kt, qoff=qoff,
                 n=n, sb=sb, hbase=hbase: t.matmul(
                     sb[:, qoff:qoff + n],
                     kps[bi][hbase:hbase + 64, kt, :],
                     qp[hbase:hbase + 64, hp, qoff:qoff + n],
                     start=True, stop=True),
                 reads=[f"kps{bi}", "qp"], writes=[sn])
        return sb, sn

    def emit_expav(hp, bi, kt, h2, sb, sn, ecnt):
        wstart = (kt // 2) * 128
        ntot = TW - wstart
        eb = expb[ecnt % 3]
        ebn = f"expb{ecnt % 3}"
        p.op("scalar", lambda sc_, hp=hp, wstart=wstart, ntot=ntot,
             sb=sb, eb=eb: sc_.activation(
                 eb[:, 0:ntot], sb[:, wstart:TW], AF.Exp,
                 bias=mcols[:, hp:hp + 1], scale=0.125),
             reads=[sn, "mcols"], writes=[ebn])
        p.op("vector", lambda v, kt=kt, eb=eb: v.tensor_mul(
            eb[:, 0:128], eb[:, 0:128], am[:, kt % 2, :]),
             reads=[ebn, "am"], writes=[ebn])
        if wstart < 512:
            p.op("tensor", lambda t, bi=bi, kt=kt, h2=h2,
                 wstart=wstart, eb=eb: t.matmul(
                     yB[2 * h2][0:65, wstart:512],
                     svs[bi][:, kt, h2 * 65:h2 * 65 + 65],
                     eb[:, 0:512 - wstart],
                     start=(kt == 0), stop=(kt == 7)),
                 reads=[f"svs{bi}x", ebn], writes=[f"yB{2 * h2}"])
            p.op("tensor", lambda t, bi=bi, kt=kt, h2=h2,
                 wstart=wstart, ntot=ntot, eb=eb: t.matmul(
                     yB[2 * h2 + 1][0:65, :],
                     svs[bi][:, kt, h2 * 65:h2 * 65 + 65],
                     eb[:, 512 - wstart:ntot],
                     start=(kt == 0), stop=False),
                 reads=[f"svs{bi}x", ebn], writes=[f"yB{2 * h2 + 1}"])
        else:
            p.op("tensor", lambda t, bi=bi, kt=kt, h2=h2,
                 wstart=wstart, ntot=ntot, eb=eb: t.matmul(
                     yB[2 * h2 + 1][0:65, wstart - 512:512],
                     svs[bi][:, kt, h2 * 65:h2 * 65 + 65],
                     eb[:, 0:ntot],
                     start=False, stop=(kt == NT - 1)),
                 reads=[f"svs{bi}x", ebn], writes=[f"yB{2 * h2 + 1}"])

    def norm_group(hp, h2, ci, ybi):
        def emit(bb, bbn):
            slot = 2 * h2 + ci
            hbase = h2 * 64
            p.op("scalar", lambda sc_, slot=slot, ybi=ybi: sc_.activation(
                rrf[0:1, :], ybig[ybi][64:65, slot, :], AF.Ln),
                 reads=[f"ybig{ybi}"], writes=["rrf"])
            p.op("scalar", lambda sc_: sc_.activation(
                rr[0:1, :], rrf[0:1, :], AF.Exp, bias=0.0, scale=-1.0),
                 reads=["rrf"], writes=["rr"])
            p.op("tensor", lambda t, bb=bb: t.matmul(
                bb[0:64, 0:512], ones_row[0:1, 0:64], rr[0:1, :],
                start=True, stop=True),
                 reads=["rr", "ones_row"], writes=[bbn])
            p.op("vector", lambda v, hp=hp, hbase=hbase, ci=ci, slot=slot,
                 bb=bb, ybi=ybi: v.tensor_mul(
                     yb[hbase:hbase + 64, hp, ci * 512:ci * 512 + 512],
                     ybig[ybi][0:64, slot, :], bb[0:64, 0:512]),
                 reads=[f"ybig{ybi}", bbn], writes=["yb"])
        return emit

    for hp in range(OC):
        bi = hp % 2
        for kt in range(NT):
            p.op("sync", lambda s, hp=hp, bi=bi, kt=kt: s.dma_start(
                out=kps[bi][:, kt, :], in_=kp_dram[hp, kt]),
                 writes=[f"kps{bi}"], sem="dio", inck=16)
            p.op("sync", lambda s, hp=hp, bi=bi, kt=kt: s.dma_start(
                out=svs[bi][:, kt, :], in_=svt_dram[hp, kt]),
                 writes=[f"svs{bi}x"], sem="dio", inck=16)
        units = [(kt, h2) for kt in range(NT) for h2 in range(2)]
        inflight = [emit_scores(hp, bi, *units[0]),
                    emit_scores(hp, bi, *units[1])]
        for i, u in enumerate(units):
            sb, sn = inflight[i]
            emit_expav(hp, bi, u[0], u[1], sb, sn, i)
            if i % 8 == 5 and pend_norm:
                pend_norm.pop(0)(sb, sn)
            if i + 2 < len(units):
                inflight.append(emit_scores(hp, bi, *units[i + 2]))
            router_chunk()
        ybi = hp % 2
        for slot in range(4):
            p.op("scalar", lambda sc_, slot=slot, ybi=ybi: sc_.activation(
                ybig[ybi][0:65, slot, :], yB[slot][0:65, :], AF.Identity),
                 reads=[f"yB{slot}"], writes=[f"ybig{ybi}"])
        for h2 in range(2):
            for ci in range(2):
                pend_norm.append(norm_group(hp, h2, ci, ybi))
    while pend_norm:
        pend_norm.pop(0)(sc[0], "sc0")
        if pend_norm:
            pend_norm.pop(0)(sc[1], "sc1")
    p.barrier()

    sfree("yB3", "yB2", "yB1", "yB0", "sc1", "sc0")
    pbn = {i: palloc(f"pbx{i}", [128, 512], F32) for i in (0, 1, 4, 5)}
    pTn = palloc("pTx", [128, 1024], BF16)
    pSn = palloc("pSx", [128, 512], F32)

    p.op("vector", lambda v: v.tensor_copy(raccb[:, :], racc[:, :]),
         reads=["racc"], writes=["raccb"])
    p.op("tensor", lambda t: t.matmul(pSn[0:1, 0:512], ones_col[:, :],
                                      raccb[:, :], start=True, stop=True),
         reads=["raccb", "ones_col"], writes=["pS"])
    p.op("vector", lambda v: v.tensor_copy(mrow2[0:1, 0:512], pSn[0:1, 0:512]),
         reads=["pS"], writes=["mrowx"])
    p.op("vector", lambda v: v.tensor_add(
        mrow2[0:1, 0:256].rearrange("p (g x) -> p g x", x=2),
        mrow2[0:1, 0:512].rearrange("p (g x) -> p g x", x=4)[:, :, 0:2],
        mrow2[0:1, 0:512].rearrange("p (g x) -> p g x", x=4)[:, :, 2:4]),
         reads=["mrowx"], writes=["mrowx"])
    p.op("vector", lambda v: v.tensor_add(
        mrow2[0:1, 0:128].rearrange("p (g x) -> p g x", x=1),
        mrow2[0:1, 0:256].rearrange("p (g x) -> p g x", x=2)[:, :, 0:1],
        mrow2[0:1, 0:256].rearrange("p (g x) -> p g x", x=2)[:, :, 1:2]),
         reads=["mrowx"], writes=["mrowx"])
    p.op("vector", lambda v: v.tensor_add(mrow2[0:1, 0:64], mrow2[0:1, 0:64],
                                          mrow2[0:1, 64:128]),
         reads=["mrowx"], writes=["mrowx"])
    p.op("vector", lambda v: v.tensor_add(mrow2[0:1, 0:32], mrow2[0:1, 0:32],
                                          mrow2[0:1, 32:64]),
         reads=["mrowx"], writes=["mrowx"])
    p.op("vector", lambda v: v.tensor_add(mrow2[0:1, 0:16], mrow2[0:1, 0:16],
                                          mrow2[0:1, 16:32]),
         reads=["mrowx"], writes=["mrowx"])
    p.op("vector", lambda v: v.tensor_add(lgrow[0:1, :], mrow2[0:1, 0:8],
                                          mrow2[0:1, 8:16]),
         reads=["mrowx"], writes=["lgrow"])

    if DBG:
        dbq = salloc("dbq", [128, TW], BF16, side="right")
        for oc in range(OC):
            p.op("vector", lambda v, oc=oc: v.tensor_copy(dbq[:, :],
                                                          qp[:, oc, :]),
                 reads=["qp"], writes=["dbq"])
            p.op("gpsimd", lambda g, oc=oc: g.dma_start(out=dbg_qp[oc],
                                                        in_=dbq[:, :]),
                 reads=["dbq"], writes=["dbgo"], sem="dw", inck=16)
        sfree("dbq")
    sfree("rrf", "mrow2", "raccb", "racc", "swp", "xnl1", "xnl0", "swl1",
          "swl0", "ybig1", "ybig0", "expb2", "expb1", "expb0", "svs1", "svs0",
          "kps1", "kps0")
    p.barrier()
    sfree("qp")

    s_sb = salloc("s_sb", [128, DC, TW], BF16, side="right")
    wcb = [salloc(f"wcb{i}", [128, OC, 128], BF16, side="right")
           for i in range(2)]
    for doc in range(DC):
        wi = doc % 2
        p.op("gpsimd", lambda g, doc=doc, wi=wi: g.dma_start(
            out=wcb[wi][:, :, :], in_=wo_b[doc]),
             writes=[f"wcb{wi}"], sem="dw", inck=16)
        for tch in range(2):
            sl = slice(tch * 512, tch * 512 + 512)
            bk = tch % 2
            for oc in range(OC):
                p.op("tensor", lambda t, oc=oc, sl=sl, bk=bk, wi=wi: t.matmul(
                    pbn[bk][:, :], wcb[wi][:, oc, :],
                    yb[:, oc, sl], start=(oc == 0), stop=(oc == OC - 1)),
                     reads=[f"wcb{wi}", "yb"], writes=[f"pb{bk}"])
            p.op("scalar", lambda sc, doc=doc, sl=sl, bk=bk: sc.activation(
                s_sb[:, doc, sl], pbn[bk][:, :], AF.Identity,
                bias=bos[:, doc:doc + 1], scale=1.0),
                 reads=[f"pb{bk}", "bos"], writes=["s_sb"])
    p.barrier()
    sfree("wcb1", "wcb0")

    p.op("vector", lambda v: v.tensor_add(lgrow[0:1, :], lgrow[0:1, :],
                                          swbs[0:1, :]),
         reads=["lgrow", "swbs"], writes=["lgrow"])
    p.op("vector", lambda v: v.reduce_max(psc2[0:1, 0:1], lgrow[0:1, :], AX.X),
         reads=["lgrow"], writes=["psc2"])
    p.op("vector", lambda v: v.tensor_scalar(probs[0:1, :], lgrow[0:1, :],
                                             psc2[0:1, 0:1], None, OP.subtract),
         reads=["lgrow", "psc2"], writes=["probs"])
    p.op("scalar", lambda sc: sc.activation(probs[0:1, :], probs[0:1, :],
                                            AF.Exp),
         reads=["probs"], writes=["probs"])
    p.op("vector", lambda v: v.reduce_sum(psc2[0:1, 1:2], probs[0:1, :], AX.X),
         reads=["probs"], writes=["psc2b"])
    p.op("vector", lambda v: v.reciprocal(psc2[0:1, 2:3], psc2[0:1, 1:2]),
         reads=["psc2b"], writes=["psc2c"])
    p.op("vector", lambda v: v.tensor_scalar(probs[0:1, :], probs[0:1, :],
                                             psc2[0:1, 2:3], None, OP.mult),
         reads=["probs", "psc2c"], writes=["probs"])
    p.op("vector", lambda v: v.reduce_max(psc2[0:1, 0:1], probs[0:1, :], AX.X),
         reads=["probs"], writes=["psc2"])
    p.op("vector", lambda v: v.tensor_scalar(pscr[0:1, :], probs[0:1, :],
                                             psc2[0:1, 0:1], None, OP.is_ge),
         reads=["probs", "psc2"], writes=["pscr"])
    p.op("vector", lambda v: v.scalar_tensor_tensor(gates[0:1, :], pscr[0:1, :],
                                                    -BIG, probs[0:1, :],
                                                    OP.mult, OP.add),
         reads=["pscr", "probs"], writes=["gates"])
    p.op("vector", lambda v: v.reduce_max(psc2[0:1, 1:2], gates[0:1, :], AX.X),
         reads=["gates"], writes=["psc2b"])
    p.op("vector", lambda v: v.tensor_scalar(gates[0:1, :], gates[0:1, :],
                                             psc2[0:1, 1:2], None, OP.is_ge),
         reads=["gates", "psc2b"], writes=["gates"])
    p.op("vector", lambda v: v.tensor_add(pscr[0:1, :], pscr[0:1, :],
                                          gates[0:1, :]),
         reads=["pscr", "gates"], writes=["pscr"])
    p.op("vector", lambda v: v.tensor_mul(gates[0:1, :], probs[0:1, :],
                                          pscr[0:1, :]),
         reads=["probs", "pscr"], writes=["gates"])
    p.op("tensor", lambda t: t.matmul(pSn[:, 0:8], ones_row_f[0:1, :],
                                      gates[0:1, :], start=True, stop=True),
         reads=["gates", "ones_row_f"], writes=["pS"])
    p.op("vector", lambda v: v.tensor_copy(gbc[:, :], pSn[:, 0:8]),
         reads=["pS"], writes=["gbc"])
    p.op("tensor", lambda t: t.matmul(pSn[0:8, 8:9], gates[0:1, :],
                                      ones_row_f[0:1, 0:1], start=True,
                                      stop=True),
         reads=["gates", "ones_row_f"], writes=["pS"])
    p.op("vector", lambda v: v.tensor_copy(gcol[:, :], pSn[0:8, 8:9]),
         reads=["pS"], writes=["gcol"])
    p.op("tensor", lambda t: t.matmul(pSn[:, 9:10], b2ts[:, :], gcol[:, :],
                                      start=True, stop=True),
         reads=["b2ts", "gcol"], writes=["pS"])
    p.op("vector", lambda v: v.tensor_copy(b2g[:, :], pSn[:, 9:10]),
         reads=["pS"], writes=["b2g"])

    u_sb = salloc("u_sb", [128, DC, TW], BF16, side="right")
    w1s = salloc("w1s", [128, EN, 128], BF16, side="right")
    w2s = salloc("w2s", [128, EN, 128], BF16, side="right")
    ghs = salloc("ghs", [128, EN, 512], BF16, side="right")
    p.op("gpsimd", lambda g: g.dma_start(out=w1s[:, :, :], in_=w1t_d),
         writes=["w1s"], sem="dw", inck=16)
    p.op("gpsimd", lambda g: g.dma_start(out=w2s[:, :, :], in_=w2t_d),
         writes=["w2s"], sem="dw", inck=16)
    for e in range(EN):
        p.op("vector", lambda v, e=e: v.tensor_scalar(
            w2s[:, e, :], w2s[:, e, :], gbc[:, e:e + 1], None, OP.mult),
             reads=["w2s", "gbc"], writes=["w2s"])

    for hh in range(DC):
        for tch in range(2):
            sl = slice(tch * 512, tch * 512 + 512)
            hb = tch % 2
            ob = 4 + (tch % 2)
            for e in range(EN):
                p.op("tensor", lambda t, hh=hh, e=e, sl=sl, hb=hb: t.matmul(
                    pbn[hb][:, :], w1s[:, e, :], xn_ffn[:, hh, sl],
                    start=True, stop=True),
                     reads=["w1s", "xn_ffn"], writes=[f"pb{hb}"])
                p.op("scalar", lambda sc, e=e, hb=hb: sc.activation(
                    ghs[:, e, :], pbn[hb][:, :], AF.Gelu,
                    bias=b1s[:, e:e + 1], scale=1.0),
                     reads=[f"pb{hb}", "b1s"], writes=["ghs"])
            for e in range(EN):
                p.op("tensor", lambda t, e=e, ob=ob: t.matmul(
                    pbn[ob][:, :], w2s[:, e, :], ghs[:, e, :],
                    start=(e == 0), stop=(e == EN - 1)),
                     reads=["w2s", "ghs"], writes=[f"pb{ob}"])
            p.op("vector", lambda v, hh=hh, sl=sl, ob=ob: v.scalar_tensor_tensor(
                u_sb[:, hh, sl], pbn[ob][:, :], b2g[:, :], xn_ffn[:, hh, sl],
                OP.add, OP.add),
                 reads=[f"pb{ob}", "b2g", "xn_ffn"], writes=["u_sb"])
    p.barrier()
    sfree("ghs", "w2s", "w1s")
    sfree("xn_ffn")

    g_sb = salloc("g_sb", [128, DC, TW], BF16, side="right")
    w3cb = [salloc(f"w3cb{i}", [128, DC, 128], BF16, side="right")
            for i in range(2)]
    for doc in range(DC):
        wi = doc % 2
        p.op("gpsimd", lambda g, doc=doc, wi=wi: g.dma_start(
            out=w3cb[wi][:, :, :], in_=w3t_d[doc]),
             writes=[f"w3cb{wi}"], sem="dw", inck=16)
        for tch in range(2):
            sl = slice(tch * 512, tch * 512 + 512)
            bk = tch % 2
            for dc in range(DC):
                p.op("tensor", lambda t, dc=dc, sl=sl, bk=bk, wi=wi: t.matmul(
                    pbn[bk][:, :], w3cb[wi][:, dc, :],
                    u_sb[:, dc, sl], start=(dc == 0), stop=(dc == DC - 1)),
                     reads=[f"w3cb{wi}", "u_sb"], writes=[f"pb{bk}"])
            p.op("scalar", lambda sc, doc=doc, sl=sl, bk=bk: sc.activation(
                g_sb[:, doc, sl], pbn[bk][:, :], AF.Silu,
                bias=b3s[:, doc:doc + 1], scale=1.0),
                 reads=[f"pb{bk}", "b3s"], writes=["g_sb"])
    p.barrier()
    sfree("w3cb1", "w3cb0")

    sg_sb = salloc("sg_sb", [128, DC, TW], BF16, side="right")
    for doc in range(DC):
        p.op("vector", lambda v, doc=doc: v.tensor_mul(
            sg_sb[:, doc, :], s_sb[:, doc, :], g_sb[:, doc, :]),
             reads=["s_sb", "g_sb"], writes=["sg_sb"])
    xrb = salloc("xrb", [128, D], F32, side="right")
    otb = [salloc(f"otb{i}", [128, D], F32, side="right") for i in range(2)]
    sgt = salloc("sgt", [128, D], BF16, side="right")
    x_ffn_t = x_ffn.rearrange("(n p) d -> n p d", p=128)
    for i in range(NTH):
        bi = i % 2
        p.op("sync", lambda s, i=i: s.dma_start(out=xrb[:, :], in_=x_ffn_t[i]),
             writes=["xrb"], sem="dio", inck=16)
        for half in range(2):
            for j in range(8):
                dc = half * 8 + j
                p.op("tensor", lambda t, i=i, dc=dc, j=j: t.transpose(
                    pTn[:, j * 128:(j + 1) * 128],
                    sg_sb[:, dc, i * 128:(i + 1) * 128], i128[:, :]),
                     reads=["sg_sb", "i128"], writes=["pT"])
            p.op("vector", lambda v, half=half: v.tensor_copy(
                sgt[:, half * 1024:half * 1024 + 1024], pTn[:, :]),
                 reads=["pT"], writes=["sgt"])
        p.op("vector", lambda v, bi=bi: v.tensor_add(otb[bi][:, :], xrb[:, :],
                                                     sgt[:, :]),
             reads=["xrb", "sgt"], writes=[f"otb{bi}"])
        p.op("sync", lambda s, i=i, bi=bi: s.dma_start(out=out_d[i],
                                                       in_=otb[bi][:, :]),
             reads=[f"otb{bi}"], writes=["out_d"], sem="dio", inck=16)
    if DBG:
        dbx = salloc("dbx", [128, TW], BF16, side="right")

        def dump(dst_ap, src_ap, names):
            n = src_ap.shape[-1]
            p.op("vector", lambda v: v.tensor_copy(dbx[:, 0:n], src_ap),
                 reads=list(names), writes=["dbx"])
            p.op("gpsimd", lambda g: g.dma_start(out=dst_ap, in_=dbx[:, 0:n]),
                 reads=["dbx"], writes=["dbgo"], sem="dw", inck=16)

        for oc in range(OC):
            dump(dbg_yb[oc], yb[:, oc, :], ["yb"])
        for st in range(NT):
            p.op("gpsimd", lambda g, st=st: g.dma_start(
                out=dbg_xn[st], in_=xt_dram[st]),
                 reads=["xt_dram"], writes=["dbgo"], sem="dw", inck=16)
        for oc in range(OC):
            for kt in range(NT):
                p.op("gpsimd", lambda g, oc=oc, kt=kt: g.dma_start(
                    out=dbg_kp[oc, kt], in_=kp_dram[oc, kt]),
                     reads=["kp_dram"], writes=["dbgo"], sem="dw", inck=16)
                p.op("gpsimd", lambda g, oc=oc, kt=kt: g.dma_start(
                    out=dbg_sv[oc, kt], in_=svt_dram[oc, kt]),
                     reads=["svt_dram"], writes=["dbgo"], sem="dw", inck=16)
        for doc in range(DC):
            dump(dbg_s[doc], s_sb[:, doc, :], ["s_sb"])
            dump(dbg_g[doc], g_sb[:, doc, :], ["g_sb"])
        p.op("vector", lambda v: v.tensor_copy(dbx[:, 0:OC], mcols[:, :]),
             reads=["mcols"], writes=["dbx"])
        p.op("vector", lambda v: v.tensor_copy(dbx[:, 32:64], msc[:, :]),
             reads=["msc"], writes=["dbx"])
        p.op("vector", lambda v: v.tensor_copy(dbx[0:1, 64:72], lgrow[0:1, :]),
             reads=["lgrow"], writes=["dbx"])
        p.op("vector", lambda v: v.tensor_copy(dbx[0:1, 72:80], gates[0:1, :]),
             reads=["gates"], writes=["dbx"])
        p.op("vector", lambda v: v.tensor_copy(dbx[0:1, 80:88], probs[0:1, :]),
             reads=["probs"], writes=["dbx"])
        p.op("gpsimd", lambda g: g.dma_start(out=dbg_sc, in_=dbx[:, 0:256]),
             reads=["dbx"], writes=["dbgo"], sem="dw", inck=16)
        p.op("gpsimd", lambda g: g.dma_start(out=dbg_tab[0], in_=cos_t[:, :]),
             reads=["cos_t"], writes=["dbgo"], sem="dw", inck=16)
        p.op("gpsimd", lambda g: g.dma_start(out=dbg_tab[1], in_=sin_t[:, :]),
             reads=["sin_t"], writes=["dbgo"], sem="dw", inck=16)
        p.op("gpsimd", lambda g: g.dma_start(out=dbg_tab[2][:, 0:TW],
                                             in_=cos_q[:, :]),
             reads=["cos_q"], writes=["dbgo"], sem="dw", inck=16)
        p.op("gpsimd", lambda g: g.dma_start(out=dbg_tab[3][:, 0:TW],
                                             in_=sin_q[:, :]),
             reads=["sin_q"], writes=["dbgo"], sem="dw", inck=16)
        p.op("gpsimd", lambda g: g.dma_start(out=dbg_pm[0], in_=pmat[:, :]),
             reads=["pmat"], writes=["dbgo"], sem="dw", inck=16)
        p.op("gpsimd", lambda g: g.dma_start(out=dbg_pm[1], in_=i128[:, :]),
             reads=["i128"], writes=["dbgo"], sem="dw", inck=16)
    fin = p.counts["dio"]

    sems = {}
    sem_cms = []
    for n in ("ph", "pe", "act", "dve", "pool", "dio", "dw"):
        cm = nc.semaphore(n)
        sems[n] = cm.__enter__()
        sem_cms.append(cm)
    with nc.allow_low_precision("bf16 intermediates are deliberate"), \
            nc.Block() as block:
        @block.sync
        def _(sync):
            p.emit("sync", sync, sems)
            sync.wait_ge(sems["dio"], fin)

        @block.tensor
        def _(tensor):
            p.emit("tensor", tensor, sems)

        @block.vector
        def _(vector):
            p.emit("vector", vector, sems)

        @block.scalar
        def _(scalar):
            p.emit("scalar", scalar, sems)

        @block.gpsimd
        def _(gpsimd):
            p.emit("gpsimd", gpsimd, sems)

    for name in reversed(list(live)):
        live.pop(name).__exit__(None, None, None)
    for cm in sem_cms:
        cm.__exit__(None, None, None)
    return nc


_NC_CACHE = None
_LAST_RES = None


def _get_nc():
    global _NC_CACHE
    if _NC_CACHE is None:
        _NC_CACHE = build_program()
    return _NC_CACHE


def kernel(L0_attn_x, L0_attn_Wq, L0_attn_bq, L0_attn_Wk, L0_attn_bk,
           L0_attn_Wv, L0_attn_bv, L0_attn_Wo, L0_attn_bo,
           L0_ffn_switch_w, L0_ffn_switch_b, L0_ffn_w1, L0_ffn_b1,
           L0_ffn_w2, L0_ffn_b2, L0_ffn_w3, L0_ffn_b3):
    f = np.float32
    x = np.ascontiguousarray(np.asarray(L0_attn_x, dtype=f))
    Wq = np.asarray(L0_attn_Wq, dtype=f)
    Wk = np.asarray(L0_attn_Wk, dtype=f)
    Wv = np.asarray(L0_attn_Wv, dtype=f)
    Wo = np.asarray(L0_attn_Wo, dtype=f)
    sw = np.asarray(L0_ffn_switch_w, dtype=f)
    w1 = np.asarray(L0_ffn_w1, dtype=f)
    w2 = np.asarray(L0_ffn_w2, dtype=f)
    w3 = np.asarray(L0_ffn_w3, dtype=f)

    def blkw(WT):
        return np.ascontiguousarray(
            WT.reshape(DC, 128, OC, 128).transpose(2, 1, 0, 3))

    wqb = blkw(Wq.T)
    wkb = blkw(Wk.T)
    wvb = blkw(Wv.T)
    wob = np.ascontiguousarray(
        Wo.T.reshape(OC, 128, DC, 128).transpose(2, 1, 0, 3))
    w3t = np.ascontiguousarray(
        w3.T.reshape(DC, 128, DC, 128).transpose(2, 1, 0, 3))
    w1t = np.ascontiguousarray(w1.transpose(2, 0, 1))
    w2t = np.ascontiguousarray(w2.transpose(2, 0, 1))
    swb = np.ascontiguousarray(sw.reshape(EN, NT, 128, D))
    bq_c = np.ascontiguousarray(np.asarray(L0_attn_bq, f).reshape(OC, 128).T)
    bk_c = np.ascontiguousarray(np.asarray(L0_attn_bk, f).reshape(OC, 128).T)
    bv_c = np.ascontiguousarray(np.asarray(L0_attn_bv, f).reshape(OC, 128).T)
    bo_c = np.ascontiguousarray(np.asarray(L0_attn_bo, f).reshape(DC, 128).T)
    b3_c = np.ascontiguousarray(np.asarray(L0_ffn_b3, f).reshape(DC, 128).T)
    b1_c = np.ascontiguousarray(np.asarray(L0_ffn_b1, f).T)
    b2t = np.ascontiguousarray(np.asarray(L0_ffn_b2, f))
    swb_r = np.ascontiguousarray(np.asarray(L0_ffn_switch_b, f).reshape(1, EN))
    pos = np.arange(S, dtype=f).reshape(1, S)

    in_maps = []
    for c in range(8):
        b, r = c // 2, c % 2
        own = [r + 2 * i for i in range(NTH)]
        rows = np.concatenate([np.arange(g * 128, (g + 1) * 128) for g in own])
        posq = rows.astype(f).reshape(1, TW)
        in_maps.append({
            "x_attn": x[b],
            "x_ffn": np.ascontiguousarray(x[b][rows, :]),
            "wq_b": wqb, "wk_b": wkb, "wv_b": wvb,
            "wo_b": wob, "w3t": w3t, "w1t": w1t, "w2t": w2t,
            "sw_b": swb,
            "bq_c": bq_c, "bk_c": bk_c, "bv_c": bv_c,
            "bo_c": bo_c, "b3_c": b3_c, "b1_c": b1_c, "b2t": b2t,
            "swb": swb_r, "pos": pos, "posq": posq,
            "parc": np.full((128, 1), float(r), f),
        })

    nc = _get_nc()
    res = run_bass_kernel_spmd(nc, in_maps, core_ids=list(range(8)))
    global _LAST_RES
    _LAST_RES = res
    out = np.empty((4, S, D), dtype=np.float32)
    for c in range(8):
        b, r = c // 2, c % 2
        for i in range(NTH):
            g = r + 2 * i
            out[b, g * 128:(g + 1) * 128, :] = res.results[c]["out"][i]
    return out


if __name__ == "__main__":
    build_program()
    print("build ok")
